# version 25 (final)
# baseline (speedup 1.0000x reference)
import contextlib
import os

import numpy as np

import concourse.bass as bass
import concourse.mybir as mybir
from concourse.bass_utils import run_bass_kernel_spmd

F32 = mybir.dt.float32
BF16 = mybir.dt.bfloat16
I32 = mybir.dt.int32
AX = mybir.AxisListType.X
ALU = mybir.AluOpType
ACT = mybir.ActivationFunctionType

B = 8
DIMS = dict(S=2048, E=1024, I=2816)
TRUNC_CVT = [os.environ.get("BASS_SIM_FP64") == "1"]


def _set_dims(S, E, I):
    DIMS.update(S=S, E=E, I=I)


class _C:
    def __init__(self, h):
        self.h = h
        self.n = 0

    def inc(self, inst, k=1):
        inst.then_inc(self.h, k)
        self.n += k
        return self.n


def _chunks(total, step):
    out, off = [], 0
    while off < total:
        out.append((off, min(step, total - off)))
        off += step
    return out


def _build():
    S, E, II = DIMS["S"], DIMS["E"], DIMS["I"]
    NT, ECH, ICH = S // 128, E // 128, II // 128
    QW = min(768 if TRUNC_CVT[0] else 1024, max(E, 512))

    nc = bass.Bass("TRN2", target_bir_lowering=False, debug=False)

    x_d = nc.dram_tensor("x", [S, E], F32, kind="ExternalInput").ap()
    RRA = E // 8
    w_attn_d = {
        "q": nc.dram_tensor("wqs", [RRA, E], F32, kind="ExternalInput").ap(),
        "k": nc.dram_tensor("wks", [RRA, E], F32, kind="ExternalInput").ap(),
        "v": nc.dram_tensor("wvs", [RRA, E], F32, kind="ExternalInput").ap(),
        "p": nc.dram_tensor("wps", [RRA, E], F32, kind="ExternalInput").ap(),
    }
    WIX = {"q": 0, "k": 1, "v": 2, "p": 3}
    bp_d = nc.dram_tensor("bp", [E], F32, kind="ExternalInput").ap()
    CCG, RRD = E // 8, E // 8
    wgs_d = nc.dram_tensor("wgs", [II, CCG], F32, kind="ExternalInput").ap()
    wus_d = nc.dram_tensor("wus", [II, CCG], F32, kind="ExternalInput").ap()
    wds_d = nc.dram_tensor("wds", [RRD, II], F32, kind="ExternalInput").ap()
    out_d = nc.dram_tensor("out", [S, E], F32, kind="ExternalOutput").ap()
    scg_d = nc.dram_tensor("scg", [8, CCG, II], BF16, kind="Internal").ap()
    scu_d = nc.dram_tensor("scu", [8, CCG, II], BF16, kind="Internal").ap()
    scd_d = nc.dram_tensor("scd", [8, II, RRD], BF16, kind="Internal").ap()
    scwa_d = nc.dram_tensor("scwa", [8, 2, E, RRA], BF16, kind="Internal").ap()
    scwb_d = nc.dram_tensor("scwb", [8, 2, E, RRA], BF16, kind="Internal").ap()
    sgsl_d = nc.dram_tensor("sgsl", [CCG, II], BF16, kind="Internal").ap()
    susl_d = nc.dram_tensor("susl", [CCG, II], BF16, kind="Internal").ap()
    sdsl_d = nc.dram_tensor("sdsl", [II, RRD], BF16, kind="Internal").ap()
    swsla_d = nc.dram_tensor("swsla", [2, E, RRA], BF16, kind="Internal").ap()
    swslb_d = nc.dram_tensor("swslb", [2, E, RRA], BF16, kind="Internal").ap()
    amx_d = nc.dram_tensor("amx", [1, 4], F32, kind="Internal").ap()
    amxg_d = nc.dram_tensor("amxg", [1, 4], F32, kind="Internal").ap()
    scxq_d = nc.dram_tensor("scxq", [S, E], BF16, kind="Internal").ap()
    scbase_d = nc.dram_tensor("scbase", [S, E], F32, kind="Internal").ap()
    scpart_d = nc.dram_tensor("scpart", [S, E], F32, kind="Internal").ap()

    esL = contextlib.ExitStack()

    def sbL(es, name, shape, dt):
        return es.enter_context(nc.sbuf_tensor(name, shape, dt, side="left"))

    def sbR(es, name, shape, dt):
        return es.enter_context(nc.sbuf_tensor(name, shape, dt, side="right"))

    stat = sbL(esL, "stat", [128, 32], F32)
    bn12 = sbL(esL, "bn12", [128, 12], F32)
    ones = sbL(esL, "ones", [1, 128], F32)
    ident = sbL(esL, "ident", [128, 128], F32)
    C_MEAN, C_VAR, C_RSTD = 0, 1, 2
    C_EBIAS, C_D, C_RD = 4, 5, 6
    C_T0, C_T1, C_T2, C_T3 = 8, 9, 10, 11
    C_RS = {"q": 12, "k": 13, "v": 14, "p": 15}
    C_ALPHA, C_SVSP = 16, 17
    C_S = {"q": 20, "k": 21, "v": 22, "p": 23}
    C_GM = 24
    C_MN = 26
    C_EB = 4
    C_DD = 6
    C_RD2 = 28


    def sem(name):
        return _C(esL.enter_context(nc.semaphore(name)))

    sV, sS, sP, sG = sem("sV"), sem("sS"), sem("sP"), sem("sG")
    sIn, sWr, sT, sB = sem("sIn"), sem("sWr"), sem("sT"), sem("sB")
    sl0, sl1, sw0, sw1 = sem("sl0"), sem("sl1"), sem("sw0"), sem("sw1")
    sCC = sem("sCC")
    pt0, pt1 = sem("pt0"), sem("pt1")
    swl = [sem("swl%d" % i) for i in range(7)]
    sp0 = sem("sp0")
    swl_i = [0]
    sx0, sx1 = sem("sx0"), sem("sx1")
    sxp = [None, None]
    lsem = [sl0, sl1]
    wsem = [sw0, sw1]

    V, SC, T, SY, GP = nc.vector, nc.scalar, nc.tensor, nc.sync, nc.gpsimd

    def barrier():
        SY.wait_ge(sIn.h, sIn.n)
        SY.wait_ge(sWr.h, sWr.n)
        SY.wait_ge(sl0.h, sl0.n)
        SY.wait_ge(sl1.h, sl1.n)
        SY.wait_ge(sw0.h, sw0.n)
        SY.wait_ge(sw1.h, sw1.n)
        SC.wait_ge(sT.h, sT.n)
        SC.wait_ge(pt0.h, pt0.n)
        SC.wait_ge(pt1.h, pt1.n)
        for eng, own in ((V, sV), (SC, sS), (T, sP), (GP, sG)):
            eng.wait_ge(own.h, own.n)
            eng.sem_inc(sB.h, 1)
            sB.n += 1
        SY.sem_inc(sB.h, 1)
        sB.n += 1
        for eng in (V, SC, T, SY, GP):
            eng.wait_ge(sB.h, sB.n)

    tp = {}
    _tp_gen = [0]

    def open_temps(es, qw=None):
        qw = qw or QW
        tp.clear()
        g = _tp_gen[0] = _tp_gen[0] + 1
        names = [("z", F32, qw), ("az", F32, qw), ("c1", F32, qw),
                 ("c2", F32, qw), ("G", F32, qw), ("ti", I32, qw),
                 ("bm", F32, qw // 16), ("scl", F32, qw // 16),
                 ("rscl", F32, qw // 16)]
        if TRUNC_CVT[0]:
            names.insert(5, ("sg", F32, qw))
        for nm, dt, w in names:
            tp[nm] = sbR(es, "t_%s%d" % (nm, g), [128, w], dt)

    def fp4_qdq(src, dst, W, pre=None, P_=128):
        nb = W // 16
        b_ = lambda ap: ap.rearrange("p (nb b) -> p nb b", b=16)
        z, az, c1, c2, G = (tp[k][0:P_, :W] for k in
                            ("z", "az", "c1", "c2", "G"))
        sg = tp["sg"][0:P_, :W] if TRUNC_CVT[0] else None
        ti = tp["ti"][0:P_, :W]
        bm, scl, rscl = (tp["bm"][0:P_, :nb], tp["scl"][0:P_, :nb],
                         tp["rscl"][0:P_, :nb])
        rscl_b = rscl.unsqueeze(2).broadcast_to([P_, nb, 16])
        scl_b = scl.unsqueeze(2).broadcast_to([P_, nb, 16])
        V.wait_ge(sV.h, sV.n)
        if pre is not None:
            V.wait_ge(pre[0], pre[1])
        n = sV.inc(V.tensor_reduce(bm, b_(src), AX, ALU.max,
                                   apply_absolute_value=True))
        V.wait_ge(sV.h, n)
        n = sV.inc(V.tensor_scalar(scl, bm, 1.0 / 6.0, 1e-12, ALU.mult,
                                   ALU.max))
        V.wait_ge(sV.h, n)
        n = sV.inc(V.reciprocal(rscl, scl))
        V.wait_ge(sV.h, n)
        n = sV.inc(V.tensor_tensor(b_(z), b_(src), rscl_b, ALU.mult))
        V.wait_ge(sV.h, n)
        n = sV.inc(V.scalar_tensor_tensor(az, z, -1.0, z, ALU.mult, ALU.max))
        if TRUNC_CVT[0]:
            n = sV.inc(V.tensor_scalar(sg, z, 0.0, None, ALU.is_ge))
        V.wait_ge(sV.h, n)
        sV.inc(V.tensor_scalar(c1, az, 2.0, -1.0, ALU.is_gt, ALU.mult))
        n = sV.inc(V.tensor_scalar(c2, az, 4.0, -0.5, ALU.is_gt, ALU.mult))
        V.wait_ge(sV.h, n)
        n = sV.inc(V.scalar_tensor_tensor(G, c1, 2.0, c2, ALU.add, ALU.add))
        V.wait_ge(sV.h, n)
        if TRUNC_CVT[0]:
            n = sV.inc(V.tensor_tensor(az, z, G, ALU.mult))
            V.wait_ge(sV.h, n)
            nti = sV.inc(V.scalar_tensor_tensor(ti, sg, 0.5, az, ALU.subtract,
                                                ALU.add))
        else:
            nti = sV.inc(V.tensor_tensor(ti, z, G, ALU.mult))
        n = sV.inc(V.scalar_tensor_tensor(G, c2, 4.0, c1, ALU.mult, ALU.add))
        V.wait_ge(sV.h, n)
        n = sV.inc(V.tensor_scalar(G, G, -0.5, 0.5, ALU.mult, ALU.add))
        V.wait_ge(sV.h, n)
        n = sV.inc(V.tensor_tensor(z, ti, G, ALU.mult))
        V.wait_ge(sV.h, n)
        n = sV.inc(V.tensor_tensor(b_(dst), b_(z), scl_b, ALU.mult))
        return n

    with nc.Block() as block:

        @block.sync
        def _(_sp):
            esR = contextlib.ExitStack()
            open_temps(esR)
            mini_es = contextlib.ExitStack()
            p3b_wsa = sbR(mini_es, "p3b_wsa", [128, E], F32)

            g = sG.inc(GP.memset(ident[:], 0.0))
            GP.wait_ge(sG.h, g)
            sG.inc(GP.affine_select(out=ident[:], in_=ident[:],
                                    compare_op=ALU.not_equal, fill=1.0, base=0,
                                    pattern=[[-1, 128]], channel_multiplier=1))
            sG.inc(GP.memset(ones[:], 1.0))

            es3a = contextlib.ExitStack()
            ps3a = es3a.enter_context(nc.psum_tensor("ps3a", [128, 128], F32))
            wsa = p3b_wsa[:]
            for wi, key in enumerate(("v", "k", "q", "p")):
                if wi > 0:
                    SY.wait_ge(sV.h, sV.n)
                tgt = sIn.inc(SY.dma_start(out=wsa[0:RRA, :],
                                           in_=w_attn_d[key][0:RRA, :]), 16)
                V.wait_ge(sIn.h, tgt)
                if RRA < 128:
                    n = sV.inc(V.memset(stat[:, C_GM:C_GM + 1], 0.0))
                    V.wait_ge(sV.h, n)
                n = sV.inc(V.tensor_reduce(stat[0:RRA, C_GM:C_GM + 1],
                                           wsa[0:RRA, :], AX, ALU.max,
                                           apply_absolute_value=True))
                T.wait_ge(sV.h, n)
                T.wait_ge(sG.h, 2)
                npe = sP.inc(T.transpose(ps3a[0:1, :], stat[:, C_GM:C_GM + 1],
                                         ident[:]))
                V.wait_ge(sP.h, npe)
                n = sV.inc(V.tensor_reduce(stat[0:1, 20 + wi:21 + wi],
                                           ps3a[0:1, :], AX, ALU.max))
            SY.wait_ge(sV.h, sV.n)
            SY.wait_ge(sIn.h, sIn.n)
            namx = sWr.inc(SY.dma_start(out=amx_d, in_=stat[0:1, 20:24],
                                        allow_slow_non_contiguous=True), 16)
            GP.wait_ge(sWr.h, namx)
            sCC.inc(GP.collective_compute(
                "AllReduce", ALU.max, replica_groups=[list(range(8))],
                ins=[amx_d], outs=[amxg_d]))
            es3a.close()
            mini_es.close()

            RING = min(12, NT)
            es4L = contextlib.ExitStack()
            qT = sbL(es4L, "qT", [128, ECH, S], BF16)
            kT = sbL(es4L, "kT", [128, ECH, S], BF16)
            vt = sbL(es4L, "vt", [128, NT, E], BF16)
            es_xnT = contextlib.ExitStack()
            xnTr = sbL(es_xnT, "xnTr", [128, RING, ECH, 128], BF16)
            esM = contextlib.ExitStack()
            xnb_st = [sbR(esM, "xnb0", [128, E], BF16),
                      sbR(esM, "xnb1", [128, E], BF16)]
            xq_st = [sbR(esM, "xq0", [128, E], BF16),
                     sbR(esM, "xq1", [128, E], BF16)]
            wTcs = [sbR(esM, "wTc0", [128, ECH, 8, RRA], BF16),
                    sbR(esM, "wTc1", [128, ECH, 8, RRA], BF16)]
            xin_b = sbR(esM, "xin", [128, E], F32)
            xnf_b = sbR(esM, "xnf", [128, E], F32)
            xin = [xin_b, xin_b]
            xnf = xnf_b[:]
            esMP = contextlib.ExitStack()
            pjv = [esMP.enter_context(nc.psum_tensor("pjv%d" % k, [128, E], F32))
                   for k in range(2)]
            pkq = [esMP.enter_context(nc.psum_tensor("pkq%d" % k, [128, 512], F32))
                   for k in range(2)]
            GIX = {"v": (0, 0, 2), "k": (0, 1, 2), "q": (1, 0, 3), "p": (1, 1, 3)}

            def load_wT(buf_i, key, guard=0):
                gi, ki, cc = GIX[key]
                ws = swl[swl_i[0]]
                swl_i[0] += 1
                GP.wait_ge(sCC.h, cc)
                GP.wait_ge(sP.h, guard)
                src = (scwa_d, scwb_d)[gi][:, ki]
                n = 0
                for r in range(8):
                    if r == 4:
                        GP.wait_ge(ws.h, n)
                    n = ws.inc(GP.dma_start(
                        out=wTcs[buf_i][:, :, r, :],
                        in_=src[r].rearrange("(ec p) o -> p ec o", p=128)), 16)
                return (ws, n)

            def emit_p3b():
                es3P = contextlib.ExitStack()
                bc3 = es3P.enter_context(nc.psum_tensor("bc3", [128, 4], F32))
                SY.wait_ge(sCC.h, 1)
                namg = sIn.inc(SY.dma_start(out=stat[0:1, 24:28], in_=amxg_d,
                                                    allow_slow_non_contiguous=True), 16)
                T.wait_ge(sIn.h, namg)
                T.wait_ge(sG.h, 3)
                npe = sP.inc(T.matmul(bc3[:, 0:4], ones[:], stat[0:1, 24:28],
                                          start=True, stop=True))
                V.wait_ge(sP.h, npe)
                n = sV.inc(V.reciprocal(stat[:, 12:16], bc3[:, 0:4]))
                V.wait_ge(sV.h, n)
                n = sV.inc(V.tensor_scalar(stat[:, 12:16], stat[:, 12:16],
                                                   127.0, None, ALU.mult))
                n = sV.inc(V.tensor_scalar(stat[:, 20:24], bc3[:, 0:4],
                                                   1.0 / 127.0, None, ALU.mult))
                C_RS = {"v": 12, "k": 13, "q": 14, "p": 15}
                C_S = {"v": 20, "k": 21, "q": 22, "p": 23}
                V.wait_ge(sV.h, n)
                n = sV.inc(V.tensor_tensor(stat[:, C_ALPHA:C_ALPHA + 1],
                                                   stat[:, C_S["q"]:C_S["q"] + 1],
                                                   stat[:, C_S["k"]:C_S["k"] + 1], ALU.mult))
                V.wait_ge(sV.h, n)
                sV.inc(V.tensor_scalar(stat[:, C_ALPHA:C_ALPHA + 1],
                                           stat[:, C_ALPHA:C_ALPHA + 1], 0.125, None,
                                           ALU.mult))
                sV.inc(V.tensor_tensor(stat[:, C_SVSP:C_SVSP + 1],
                                           stat[:, C_S["v"]:C_S["v"] + 1],
                                           stat[:, C_S["p"]:C_S["p"] + 1], ALU.mult))
                wsa3 = xnf[:]
                wqst = tp["c1"][:].bitcast(BF16)[:, 0:E]
                wslt = tp["c2"][:].bitcast(BF16)[:, 0:ECH * RRA]
                last_wsl_store = [None]
                for gi, group in enumerate((("v", "k"), ("q", "p"))):
                    gsem = (pt0, pt1)[gi]
                    for ki, key in enumerate(group):
                        SY.wait_ge(sV.h, sV.n)
                        SY.wait_ge(sT.h, sT.n)
                        tgt = sIn.inc(SY.dma_start(out=wsa3[0:RRA, :],
                                                           in_=w_attn_d[key][0:RRA, :]), 16)
                        V.wait_ge(sIn.h, tgt)
                        V.wait_ge(gsem.h, gsem.n)
                        rs_ap = stat[0:RRA, C_RS[key]:C_RS[key] + 1]
                        for qoff, qw in _chunks(E, QW):
                            tz = tp["z"][0:RRA, :qw]
                            tti = tp["ti"][0:RRA, :qw]
                            src = wsa3[0:RRA, qoff:qoff + qw]
                            V.wait_ge(sV.h, sV.n)
                            if TRUNC_CVT[0]:
                                tsg = tp["sg"][0:RRA, :qw]
                                sV.inc(V.tensor_scalar(tsg, src, 0.0, None,
                                                       ALU.is_ge))
                                n2 = sV.inc(V.tensor_scalar(tz, src, rs_ap,
                                                            None, ALU.mult))
                                V.wait_ge(sV.h, n2)
                                n = sV.inc(V.scalar_tensor_tensor(
                                    tti, tsg, 0.5, tz, ALU.subtract, ALU.add))
                            else:
                                n = sV.inc(V.tensor_scalar(tti, src, rs_ap,
                                                           None, ALU.mult))
                            V.wait_ge(sV.h, n)
                            n = sV.inc(V.tensor_copy(
                                wqst[0:RRA, qoff:qoff + qw], tti))
                        SC.wait_ge(sV.h, n)
                        if last_wsl_store[0] is not None:
                                SC.wait_ge(last_wsl_store[0].h, last_wsl_store[1])
                        nt_ = sT.inc(SC.dma_start_transpose(
                                wslt[:].rearrange("p (ec o) -> p ec o", o=RRA),
                                wqst[0:RRA, :].rearrange("p (ec e) -> p ec e", e=128)), 16)
                        SY.wait_ge(sT.h, nt_)
                        gsem.inc(SY.dma_start(
                                out=(swsla_d, swslb_d)[gi][ki].rearrange(
                                    "(ec p) o -> p ec o", p=128),
                                in_=wslt[:].rearrange("p (ec o) -> p ec o", o=RRA)), 16)
                        last_wsl_store[0], last_wsl_store[1:] = gsem, [gsem.n]
                    GP.wait_ge(gsem.h, gsem.n)
                    sCC.inc(GP.collective_compute(
                        "AllGather", ALU.bypass, replica_groups=[list(range(8))],
                        ins=[(swsla_d, swslb_d)[gi]], outs=[(scwa_d, scwb_d)[gi]]))
                SY.wait_ge(sIn.h, sIn.n)
                V.wait_ge(sT.h, sT.n)
                V.wait_ge(pt0.h, pt0.n)
                V.wait_ge(pt1.h, pt1.n)
                es3P.close()


            wv_ready = [0]
            kq_done = [0]
            ln_done = [0]
            copy_last = [0]
            vevac_done = [0, 0]
            ring_free = {}
            MSL_E = _chunks(E, 512)
            kq_evac = [0, 0]

            def kq_prod(proj, g, buf_i, wld):
                dstT = kT if proj == "k" else qT
                T.wait_ge(wld[0].h, wld[1])
                step = 0
                for oc in range(ECH):
                    for (s0, sw) in _chunks(S // 2, 512):
                        pk = pkq[step % 2]
                        T.wait_ge(sS.h, kq_evac[step % 2])
                        a0 = g * (S // 2) + s0
                        rs = (a0 // 128) % RING
                        last = None
                        for ec in range(ECH):
                            st = (wTcs[buf_i][:, ec, oc, :] if RRA == 128 else
                                  wTcs[buf_i][:, ec, oc * (128 // RRA):(oc + 1) * (128 // RRA), :])
                            last = T.matmul(
                                pk[:, 0:sw], st,
                                xnTr[:, rs:rs + sw // 128, ec, :],
                                start=(ec == 0), stop=(ec == ECH - 1))
                        npe = sP.inc(last)
                        SC.wait_ge(sP.h, npe)
                        kq_evac[step % 2] = sS.inc(
                            SC.activation(dstT[:, oc, a0:a0 + sw],
                                          pk[:, 0:sw], ACT.Copy))
                        step += 1
                kq_done[0] = sP.n

            emit_p3b()
            wv_ready[0] = load_wT(0, "v")
            if NT > 9:
                sched = {7: [("k", 0)], 9: [("q", 0)]}
            else:
                sched = {NT - 1: [("k", 0), ("q", 0)]}
            xload_done = [0] * NT

            def xload(t):
                if t >= 1:
                    SY.wait_ge(sV.h, ln_done[0])
                xload_done[t] = lsem[t % 2].inc(SY.dma_start(
                    out=xin_b[:], in_=x_d[t * 128:(t + 1) * 128, :]), 16)

            xload(0)

            def ln_stats(t):
                p = t % 2
                mv0 = 2 * p
                V.wait_ge(lsem[p].h, xload_done[t])
                sV.inc(V.bn_stats(bn12[:, 0:6], xin_b[:, 0:E // 2]))
                n = sV.inc(V.bn_stats(bn12[:, 6:12], xin_b[:, E // 2:E]))
                V.wait_ge(sV.h, n)
                n = sV.inc(V.bn_aggr(stat[:, mv0:mv0 + 2], bn12[:]))
                V.wait_ge(sV.h, n)
                n = sV.inc(V.tensor_scalar(stat[:, 8 + p:9 + p],
                                           stat[:, mv0 + 1:mv0 + 2], 1e-5,
                                           None, ALU.add))
                SC.wait_ge(sV.h, n)
                return sS.inc(SC.activation(stat[:, 10 + p:11 + p],
                                            stat[:, 8 + p:9 + p], ACT.Sqrt))

            def ln_newton(t, m):
                p = t % 2
                T0, T1, T2, T3 = (stat[:, c + p:c + p + 1]
                                  for c in (6, 8, 10, 12))
                V.wait_ge(sS.h, m)
                n = sV.inc(V.reciprocal(T0, T2))
                V.wait_ge(sV.h, n)
                n = sV.inc(V.tensor_tensor(T3, T0, T0, ALU.mult))
                V.wait_ge(sV.h, n)
                n = sV.inc(V.tensor_tensor(T3, T3, T1, ALU.mult))
                V.wait_ge(sV.h, n)
                n = sV.inc(V.tensor_scalar(T3, T3, -0.5, 1.5, ALU.mult,
                                           ALU.add))
                V.wait_ge(sV.h, n)
                n = sV.inc(V.tensor_tensor(stat[:, 4 + p:5 + p], T0, T3,
                                           ALU.mult))
                V.wait_ge(sV.h, n)

            ln_newton(0, ln_stats(0))
            for t in range(NT):
                sl = t % 2
                sxs = (sx0, sx1)[sl]
                if t >= 1:
                    V.wait_ge(sS.h, copy_last[0])
                n = sV.inc(V.tensor_scalar(xnf[:], xin_b[:],
                                           stat[:, 2 * sl:2 * sl + 1],
                                           stat[:, 4 + sl:5 + sl],
                                           ALU.subtract, ALU.mult))
                ln_done[0] = n
                if t + 1 < NT:
                    xload(t + 1)
                SC.wait_ge(sV.h, n)
                SC.wait_ge(sxs.h, sxs.n)
                m = sS.inc(SC.activation(xnb_st[sl][:], xnf[:], ACT.Copy))
                copy_last[0] = m
                SC.wait_ge(sS.h, m)
                slot = t % RING
                if slot in ring_free:
                    SC.wait_ge(sP.h, ring_free[slot])
                nxt = sxs.inc(SC.dma_start_transpose(
                    xnTr[:, slot],
                    xnb_st[sl][:].rearrange("p (ec e) -> p ec e", e=128)), 16)
                V.wait_ge(wsem[sl].h, wsem[sl].n)
                fin = 0
                for off, W in _chunks(E, QW):
                    fin = fp4_qdq(xnf[:, off:off + W], xq_st[sl][:, off:off + W], W)
                SY.wait_ge(sV.h, fin)
                wsem[sl].inc(SY.dma_start(
                    out=scxq_d[t * 128:(t + 1) * 128, :], in_=xq_st[sl][:]), 16)
                if t + 1 < NT:
                    sq_next = ln_stats(t + 1)
                pv = pjv[sl]
                T.wait_ge(wv_ready[0][0].h, wv_ready[0][1])
                T.wait_ge(sxs.h, nxt)
                T.wait_ge(sS.h, vevac_done[sl])
                last = None
                for ec in range(ECH):
                    st = xnTr[:, slot, ec, :]
                    for (o0, ow) in MSL_E:
                        last = T.matmul(pv[:, o0:o0 + ow], st,
                                        (wTcs[0][:, ec, o0 // RRA:(o0 + ow) // RRA, :]),
                                        start=(ec == 0), stop=(ec == ECH - 1))
                npe = sP.inc(last)
                SC.wait_ge(sP.h, npe)
                vevac_done[sl] = sS.inc(
                    SC.activation(vt[:, t, :], pv[:, 0:E], ACT.Copy))
                if t + 1 < NT:
                    ln_newton(t + 1, sq_next)
                for (proj, g_) in sched.get(t, []):
                    wld = load_wT(1, proj, guard=kq_done[0])
                    kq_prod(proj, g_, 1, wld)
                    if NT > RING:
                        ring_free.update({s_: kq_done[0] for s_ in range(8)})
            for (proj, g_) in (("k", 1), ("q", 1)):
                wld = load_wT(1, proj, guard=kq_done[0])
                kq_prod(proj, g_, 1, wld)
            barrier()
            esMP.close()
            esM.close()
            es_xnT.close()
            es7a = contextlib.ExitStack()
            wpTT = sbL(es7a, "wpTT", [128, ECH, 8, RRA], BF16)
            bb = sbL(es7a, "bb", [128, E], F32)
            es5P = contextlib.ExitStack()
            sco = [es5P.enter_context(nc.psum_tensor("scoA", [128, max(S, 2 * E)], F32)),
                   es5P.enter_context(nc.psum_tensor("scoB", [128, max(S, 2 * E)], F32))]
            es5R = contextlib.ExitStack()
            p_bf = [sbR(es5R, "p_bf0", [128, S], BF16),
                    sbR(es5R, "p_bf1", [128, S], BF16)]
            pTb = [sbR(es5R, "pT0", [128, NT, 128], BF16),
                   sbR(es5R, "pT1", [128, NT, 128], BF16)]
            avb = [sbR(es5R, "av0", [128, E], BF16),
                   sbR(es5R, "av1", [128, E], BF16)]
            avT_t = [sbR(es5R, "avT0", [128, ECH, 128], BF16),
                     sbR(es5R, "avT1", [128, ECH, 128], BF16)]
            xrl = [sbR(es5R, "xr0", [128, E], F32), sbR(es5R, "xr1", [128, E], F32)]
            baseb = [sbR(es5R, "bs0", [128, E], F32), sbR(es5R, "bs1", [128, E], F32)]
            wsl_in = sbR(es5R, "wsl_in", [128, QW], F32)
            wsl_out = sbR(es5R, "wsl_out", [128, QW], BF16)
            wsl_t = sbR(es5R, "wsl_t", [128, QW], BF16)
            SY.wait_ge(sCC.h, 3)
            SY.wait_ge(sIn.h, sIn.n)
            nld7 = 0
            for r in range(8):
                if r % 3 == 0 and r:
                    SY.wait_ge(sIn.h, sIn.n)
                nld7 = sIn.inc(SY.dma_start(
                    out=wpTT[:, :, r, :],
                    in_=scwb_d[r, 1].rearrange("(ec p) o -> p ec o", p=128)), 16)
            nbb = sIn.inc(SY.dma_start(out=bb[:],
                                       in_=bp_d[None, :].broadcast_to([128, E])), 16)
            nld7 = nbb
            SY.wait_ge(sIn.h, sIn.n)
            if CCG < 128:
                p0_memset = sV.inc(V.memset(wsl_in[:], 0.0))
            else:
                p0_memset = 0
            p0_units = []
            for (src_d, dst_d, kind) in ((wgs_d, sgsl_d, "col"),
                                         (wus_d, susl_d, "col"),
                                         (wds_d, sdsl_d, "row")):
                for off, W in _chunks(II, QW):
                    p0_units.append((src_d, dst_d, kind, off, W))
            p0_state = dict(ld=0, st=0, rd=p0_memset, tp=0, cur=None,
                            pend=None)

            def p0_flush():
                P_, WW, cw, ntt, dst_d, off, W, kind, fin = p0_state["pend"]
                p0_state["pend"] = None
                SC.wait_ge(sV.h, fin)
                SC.wait_ge(sWr.h, p0_state["st"])
                SC.wait_ge(sT.h, p0_state["tp"])
                p0_state["tp"] = sT.inc(SC.dma_start_transpose(
                    wsl_t[:, 0:ntt * P_].rearrange("p (nt o) -> p nt o",
                                                   nt=ntt),
                    wsl_out[0:P_, 0:WW].rearrange("p (nt c) -> p nt c",
                                                  nt=ntt)), 16)
                SY.wait_ge(sT.h, p0_state["tp"])
                if kind == "col":
                    outv = dst_d[:, off:off + W].rearrange(
                        "c (nt n) -> c nt n", n=128)
                else:
                    outv = dst_d[off:off + W, :].rearrange(
                        "(nt n) o -> n nt o", n=128)
                p0_state["st"] = sWr.inc(SY.dma_start(
                    out=outv,
                    in_=wsl_t[0:cw, 0:ntt * P_].rearrange(
                        "p (nt o) -> p nt o", nt=ntt)), 16)

            def emit_p0_unit(u):
                if p0_state["pend"]:
                    p0_flush()
                src_d, dst_d, kind, off, W = u
                ntt = W // 128
                SY.wait_ge(sV.h, p0_state["rd"])
                SY.wait_ge(sp0.h, p0_state["ld"])
                if kind == "col":
                    P_, WW, cw = 128, ntt * 128, CCG
                    inv = src_d[off:off + W, :].rearrange(
                        "(nt p) c -> p nt c", p=128)
                    ldv = wsl_in[0:128, 0:WW].rearrange(
                        "p (nt c) -> p nt c", c=128)[:, :, 0:CCG]
                else:
                    P_, WW, cw = RRD, W, 128
                    inv = src_d[0:RRD, off:off + W]
                    ldv = wsl_in[0:RRD, 0:W]
                p0_state["ld"] = sp0.inc(SY.dma_start(out=ldv, in_=inv), 16)
                V.wait_ge(sWr.h, p0_state["st"])
                V.wait_ge(sT.h, p0_state["tp"])
                fin = fp4_qdq(wsl_in[0:P_, 0:WW], wsl_out[0:P_, 0:WW], WW,
                              pre=(sp0.h, p0_state["ld"]), P_=P_)
                p0_state["rd"] = fin
                p0_state["pend"] = (P_, WW, cw, ntt, dst_d, off, W, kind, fin)

            exp_done = [0, 0]
            avevac_done = [0, 0]
            avmm_done = [0, 0]
            pT_done = [0, 0]
            rd_done = [0, 0]
            avT_done = [0, 0]
            proj_done = [0, 0]
            base_done = [0, 0]
            basewr_done = [0, 0]
            def emit_sc_part(i):
                sl = i % 2
                p = sco[sl]
                T.wait_ge(sS.h, exp_done[sl])
                T.wait_ge(sS.h, avevac_done[sl])
                T.wait_ge(sV.h, base_done[sl])
                last = None
                for oc in range(ECH):
                    st = qT[:, oc, i * 128:(i + 1) * 128]
                    for (s0, sw) in _chunks(S, 512):
                        last = T.matmul(p[:, s0:s0 + sw], st,
                                        kT[:, oc, s0:s0 + sw],
                                        start=(oc == 0), stop=(oc == ECH - 1))
                nsc = sP.inc(last)
                cMN, cEB, cDD, cRD = C_MN + sl, C_EB + sl, C_DD + sl, C_RD2 + sl
                V.wait_ge(sP.h, nsc)
                V.wait_ge(sS.h, exp_done[sl])
                n = sV.inc(V.reduce_max(stat[:, cMN:cMN + 1], p[:, 0:S], AX,
                                        negate=True))
                V.wait_ge(sV.h, n)
                n = sV.inc(V.tensor_tensor(stat[:, cEB:cEB + 1],
                                           stat[:, cMN:cMN + 1],
                                           stat[:, C_ALPHA:C_ALPHA + 1], ALU.mult))
                SC.wait_ge(sV.h, n)
                SC.wait_ge((pt0, pt1)[sl].h, pT_done[sl])
                m = sS.inc(SC.activation(p_bf[sl][:], p[:, 0:S], ACT.Exp,
                                         bias=stat[:, cEB:cEB + 1],
                                         scale=stat[:, C_ALPHA:C_ALPHA + 1],
                                         accum_out=stat[:, cDD:cDD + 1]))
                exp_done[sl] = m
                V.wait_ge(sS.h, m)
                V.wait_ge(sS.h, avevac_done[sl])
                n = sV.inc(V.reciprocal(stat[:, cRD:cRD + 1],
                                        stat[:, cDD:cDD + 1]))
                rd_done[sl] = n
                SC.wait_ge(sS.h, exp_done[sl])
                SC.wait_ge(sP.h, avmm_done[sl])
                ptsem = (pt0, pt1)[sl]
                pT_done[sl] = ptsem.inc(SC.dma_start_transpose(
                    pTb[sl][:],
                    p_bf[sl][:].rearrange("p (t s) -> p t s", s=128)), 16)
            def emit_av_part(i):
                sl = i % 2
                p = sco[sl]
                T.wait_ge(sS.h, exp_done[sl])
                T.wait_ge((pt0, pt1)[sl].h, pT_done[sl])
                T.wait_ge(sS.h, avevac_done[sl])
                last = None
                for kc in range(NT):
                    st = pTb[sl][:, kc, :]
                    for (o0, ow) in _chunks(E, 512):
                        last = T.matmul(p[:, o0:o0 + ow], st,
                                        vt[:, kc, o0:o0 + ow],
                                        start=(kc == 0), stop=(kc == NT - 1))
                nav = sP.inc(last)
                avmm_done[sl] = nav
                SC.wait_ge(sP.h, nav)
                SC.wait_ge(sV.h, rd_done[sl])
                sxs5 = (sx0, sx1)[sl]
                SC.wait_ge(sxs5.h, sxs5.n)
                avevac_done[sl] = sS.inc(SC.activation(
                    avb[sl][:], p[:, 0:E], ACT.Copy,
                    scale=stat[:, C_RD2 + sl:C_RD2 + sl + 1]))
                SC.wait_ge(sS.h, avevac_done[sl])
                SC.wait_ge(sP.h, proj_done[sl])
                avT_done[sl] = sxs5.inc(SC.dma_start_transpose(
                    avT_t[sl][:],
                    avb[sl][:].rearrange("p (t n) -> p t n", n=128)), 16)
                SY.wait_ge(sV.h, base_done[sl])
                nxr = lsem[sl].inc(SY.dma_start(
                    out=xrl[sl][:], in_=x_d[i * 128:(i + 1) * 128, :]), 16)
                T.wait_ge(sxs5.h, avT_done[sl])
                T.wait_ge(sIn.h, nld7)
                last = None
                for ec in range(ECH):
                    st = avT_t[sl][:, ec, :]
                    for (o0, ow) in _chunks(E, 512):
                        last = T.matmul(p[:, E + o0:E + o0 + ow], st,
                                        wpTT[:, ec, o0 // RRA:(o0 + ow) // RRA, :],
                                        start=(ec == 0), stop=(ec == ECH - 1))
                npr = sP.inc(last)
                proj_done[sl] = npr
                V.wait_ge(sP.h, npr)
                V.wait_ge(lsem[sl].h, nxr)
                V.wait_ge(sV.h, sV.n)
                n = sV.inc(V.scalar_tensor_tensor(
                    xrl[sl][:], p[:, E:2 * E], stat[:, C_SVSP:C_SVSP + 1],
                    xrl[sl][:], ALU.mult, ALU.add))
                V.wait_ge(sV.h, n)
                V.wait_ge(sIn.h, nbb)
                V.wait_ge(wsem[sl].h, basewr_done[sl])
                n = sV.inc(V.tensor_tensor(baseb[sl][:], xrl[sl][:], bb[:],
                                           ALU.add))
                base_done[sl] = n
                SY.wait_ge(sV.h, n)
                basewr_done[sl] = wsem[sl].inc(SY.dma_start(
                    out=scbase_d[i * 128:(i + 1) * 128, :], in_=baseb[sl][:]), 16)
            def emit_p0_batch(i):
                    rem = max(1, (NT * 3 // 4) - i)
                    nu = (len(p0_units) + rem - 1) // rem
                    for _k in range(nu):
                        if p0_units:
                            emit_p0_unit(p0_units.pop(0))
                    if not p0_units and p0_state["cur"] != "done":
                        p0_state["cur"] = "done"
                        if p0_state["pend"]:
                            p0_flush()
                        GP.wait_ge(sWr.h, sWr.n)
                        sCC.inc(GP.collective_compute(
                            "AllGather", ALU.bypass,
                            replica_groups=[list(range(8))],
                            ins=[sgsl_d], outs=[scg_d]))
                        sCC.inc(GP.collective_compute(
                            "AllGather", ALU.bypass,
                            replica_groups=[list(range(8))],
                            ins=[susl_d], outs=[scu_d]))
                        sCC.inc(GP.collective_compute(
                            "AllGather", ALU.bypass,
                            replica_groups=[list(range(8))],
                            ins=[sdsl_d], outs=[scd_d]))

            for i in range(NT + 1):
                if i < NT:
                    emit_sc_part(i)
                if i >= 1:
                    emit_av_part(i - 1)
                if i < NT:
                    emit_p0_batch(i)
            es5R.close()
            barrier()
            es5P.close()
            es7a.close()
            es4L.close()

            if (II // 2) % 128 == 0 and II > 1024:
                HALVES = [(0, II // 2), (II // 2, II // 2)]
            else:
                HALVES = [(0, II)]
            es8L = contextlib.ExitStack()
            IHX = HALVES[0][1]
            ICH2 = IHX // 128
            CH8 = [(0, IHX)]
            CW8 = IHX
            wgH = sbL(es8L, "wgH", [128, ECH, IHX], BF16)
            wuH = sbL(es8L, "wuH", [128, ECH, IHX], BF16)
            wdH = sbL(es8L, "wdH", [128, ICH2, 8, RRD], BF16)
            es8R = contextlib.ExitStack()
            xqT_t = [sbR(es8R, "xqt0", [128, ECH, 128], BF16),
                     sbR(es8R, "xqt1", [128, ECH, 128], BF16)]
            hq = [sbR(es8R, "hq0", [128, IHX], BF16),
                  sbR(es8R, "hq1", [128, IHX], BF16)]
            hqT = [sbR(es8R, "hqT0", [128, ICH2, 128], BF16),
                   sbR(es8R, "hqT1", [128, ICH2, 128], BF16)]
            sgl = [sbR(es8R, "sgl0", [128, CW8], F32),
                   sbR(es8R, "sgl1", [128, CW8], F32)]
            hbuf = sbR(es8R, "hbuf", [128, CW8], F32)
            rlb = [sbR(es8R, "rl0", [128, E], F32), sbR(es8R, "rl1", [128, E], F32)]
            stb = [sbR(es8R, "st0", [128, E], F32), sbR(es8R, "st1", [128, E], F32)]
            open_temps(es8R, qw=CW8)
            es8P = contextlib.ExitStack()
            pg = es8P.enter_context(nc.psum_tensor("pg", [128, CW8], F32))
            py = es8P.enter_context(nc.psum_tensor("py", [128, CW8], F32))
            pdn0 = es8P.enter_context(nc.psum_tensor("pdn0", [128, E], F32))
            scg_re = scg_d.rearrange("(ec rr) q n -> (rr q) ec n", ec=ECH)
            scu_re = scu_d.rearrange("(ec rr) q n -> (rr q) ec n", ec=ECH)
            xq_rd = [0, 0]
            nxq_p = [0, 0]
            wg_last = [0]
            wu_last = [0]
            wd_last = [0]
            silu_prev = [0]
            h_prev = [0]
            h_prev2 = [0, 0]
            hqtp_done = [0, 0]
            hqtp_done2 = [0, 0]
            rl_n = [0, 0]
            rl_n2 = [0, 0]
            down_done = [0, 0]
            f_done = [0, 0]
            f_last = [0]
            stwr_done = [0, 0]
            cc8 = [0]
            for hi, (IH0, IHW) in enumerate(HALVES):
                lasth = hi == len(HALVES) - 1
                rl_src = scbase_d if hi == 0 else scpart_d
                st_dst = out_d if lasth else scpart_d
                SY.wait_ge(sIn.h, sIn.n)
                SY.wait_ge(sCC.h, 6)
                SY.wait_ge(sP.h, wg_last[0])
                sIn.inc(SY.dma_start(out=wgH[:],
                                     in_=scg_re[:, :, IH0:IH0 + IHW]), 16)
                SY.wait_ge(sP.h, wu_last[0])
                sIn.inc(SY.dma_start(out=wuH[:],
                                     in_=scu_re[:, :, IH0:IH0 + IHW]), 16)
                wdsem = swl[swl_i[0]]
                swl_i[0] += 1
                GP.wait_ge(sCC.h, 6)
                GP.wait_ge(sP.h, wd_last[0])
                nwl = 0
                for r in range(8):
                    if r == 4:
                        GP.wait_ge(wdsem.h, nwl)
                    nwl = swl_i and wdsem.inc(GP.dma_start(
                        out=wdH[:, :, r, :],
                        in_=scd_d[r, IH0:IH0 + IHW, :].rearrange(
                            "(nc p) o -> p nc o", p=128)), 16)
                SY.wait_ge(sIn.h, sIn.n)
                SY.wait_ge(sw0.h, sw0.n)
                SY.wait_ge(sw1.h, sw1.n)
                T.wait_ge(sIn.h, sIn.n)
                T.wait_ge(wdsem.h, nwl)
                def emit_xqt(sc):
                    sl = sc % 2
                    sxs8 = (sx0, sx1)[sl]
                    SC.wait_ge(sxs8.h, sxs8.n)
                    SC.wait_ge(sP.h, xq_rd[sl])
                    nxq_p[sl] = sxs8.inc(SC.dma_start_transpose(
                        xqT_t[sl][:],
                        scxq_d[sc * 128:(sc + 1) * 128, :]
                        .rearrange("p (ec e) -> p ec e", e=128)), 16)

                def emit_front(sc):
                    sl = sc % 2
                    sxs8 = (sx0, sx1)[sl]
                    T.wait_ge(sxs8.h, nxq_p[sl])
                    hq_fin = 0
                    for (off, W) in CH8:
                        nw = [(off + o, min(512, W - o))
                              for o in range(0, W, 512)]
                        T.wait_ge(sS.h, silu_prev[0])
                        last = None
                        for ec in range(ECH):
                            st = xqT_t[sl][:, ec, :]
                            for (w0, ww) in nw:
                                last = T.matmul(
                                    pg[:, w0 - off:w0 - off + ww], st,
                                    wgH[:, ec, w0:w0 + ww],
                                    start=(ec == 0), stop=(ec == ECH - 1))
                        ng = sP.inc(last)
                        wg_last[0] = ng
                        T.wait_ge(sV.h, h_prev[0])
                        last = None
                        for ec in range(ECH):
                            st = xqT_t[sl][:, ec, :]
                            for (w0, ww) in nw:
                                last = T.matmul(
                                    py[:, w0 - off:w0 - off + ww], st,
                                    wuH[:, ec, w0:w0 + ww],
                                    start=(ec == 0), stop=(ec == ECH - 1))
                        ny = sP.inc(last)
                        wu_last[0] = ny
                        xq_rd[sl] = ny
                        pp = cc8[0] % 2
                        SC.wait_ge(sP.h, ng)
                        SC.wait_ge(sV.h, h_prev2[pp])
                        ns = sS.inc(SC.activation(sgl[pp][:, 0:W], pg[:, 0:W],
                                                  ACT.Silu))
                        silu_prev[0] = ns
                        V.wait_ge(sP.h, ny)
                        V.wait_ge(sS.h, ns)
                        V.wait_ge(sV.h, sV.n)
                        nh = sV.inc(V.scalar_tensor_tensor(
                            hbuf[:, 0:W], py[:, 0:W], 1.0, sgl[pp][:, 0:W],
                            ALU.mult, ALU.mult))
                        h_prev[0] = nh
                        h_prev2[pp] = nh
                        V.wait_ge(sT.h, hqtp_done[sl])
                        hq_fin = fp4_qdq(hbuf[:, 0:W], hq[sl][:, off:off + W],
                                         W)
                        cc8[0] += 1
                    if sc + 1 < NT:
                        emit_xqt(sc + 1)
                    SC.wait_ge(sV.h, hq_fin)
                    SC.wait_ge(sT.h, sT.n)
                    SC.wait_ge(sP.h, down_done[sl])
                    hqtp_done[sl] = sT.inc(SC.dma_start_transpose(
                        hqT[sl][:],
                        hq[sl][:].rearrange("p (t n) -> p t n", n=128)), 16)
                    SY.wait_ge(sV.h, f_done[sl])
                    rl_n[sl] = lsem[sl].inc(SY.dma_start(
                        out=rlb[sl][:],
                        in_=rl_src[sc * 128:(sc + 1) * 128, :]), 16)

                def emit_back(sc):
                    sl = sc % 2
                    T.wait_ge(sT.h, hqtp_done2[sl])
                    T.wait_ge(sV.h, f_last[0])
                    last = None
                    for ic in range(ICH2):
                        st = hqT[sl][:, ic, :]
                        for (o0, ow) in _chunks(E, 512):
                            last = T.matmul(
                                pdn0[:, o0:o0 + ow], st,
                                wdH[:, ic, o0 // RRD:(o0 + ow) // RRD, :],
                                start=(ic == 0), stop=(ic == ICH2 - 1))
                    ndn = sP.inc(last)
                    down_done[sl] = ndn
                    wd_last[0] = ndn
                    V.wait_ge(sP.h, ndn)
                    V.wait_ge(lsem[sl].h, rl_n2[sl])
                    V.wait_ge(wsem[sl].h, stwr_done[sl])
                    V.wait_ge(sV.h, sV.n)
                    nf = sV.inc(V.scalar_tensor_tensor(
                        stb[sl][:], pdn0[:, 0:E], 1.0, rlb[sl][:],
                        ALU.mult, ALU.add))
                    f_done[sl] = nf
                    f_last[0] = nf
                    SY.wait_ge(sV.h, nf)
                    stwr_done[sl] = wsem[sl].inc(SY.dma_start(
                        out=st_dst[sc * 128:(sc + 1) * 128, :],
                        in_=stb[sl][:]), 16)

                emit_xqt(0)
                for sc in range(NT + 1):
                    if sc < NT:
                        emit_front(sc)
                    if sc >= 1:
                        hqtp_done2[(sc - 1) % 2] = hqtp_done[(sc - 1) % 2]
                        rl_n2[(sc - 1) % 2] = rl_n[(sc - 1) % 2]
                        emit_back(sc - 1)
            SY.wait_ge(sw0.h, sw0.n)
            SY.wait_ge(sw1.h, sw1.n)
            es8P.close()
            es8R.close()
            es8L.close()

    esL.close()
    return nc


_NC_CACHE = [None]


def kernel(**inputs):
    x = np.ascontiguousarray(np.asarray(inputs["L0_attn_x"], dtype=np.float32))
    wq = np.asarray(inputs["L0_attn_w_query"], np.float32)
    wk = np.asarray(inputs["L0_attn_w_key"], np.float32)
    wv = np.asarray(inputs["L0_attn_w_value"], np.float32)
    wp = np.asarray(inputs["L0_attn_w_proj"], np.float32)
    bp = np.ascontiguousarray(np.asarray(inputs["L0_attn_b_proj"], np.float32))
    wg = np.asarray(inputs["L0_ffn_w_gate"], np.float32)
    wu = np.asarray(inputs["L0_ffn_w_up"], np.float32)
    wd = np.asarray(inputs["L0_ffn_w_down"], np.float32)
    rra = wq.shape[0] // B
    ccg, rrd = wg.shape[1] // B, wd.shape[0] // B
    if _NC_CACHE[0] is None:
        _NC_CACHE[0] = _build()
    nc = _NC_CACHE[0]
    A = np.ascontiguousarray
    in_maps = [dict(x=A(x[c]), bp=bp,
                    wqs=A(wq[c * rra:(c + 1) * rra]),
                    wks=A(wk[c * rra:(c + 1) * rra]),
                    wvs=A(wv[c * rra:(c + 1) * rra]),
                    wps=A(wp[c * rra:(c + 1) * rra]),
                    wgs=A(wg[:, c * ccg:(c + 1) * ccg]),
                    wus=A(wu[:, c * ccg:(c + 1) * ccg]),
                    wds=A(wd[c * rrd:(c + 1) * rrd]))
               for c in range(B)]
    res = run_bass_kernel_spmd(nc, in_maps, core_ids=list(range(B)))
    out = np.stack([np.asarray(res.results[c]["out"]) for c in range(B)], axis=0)
    return out.astype(np.float32)


# version 27
# speedup vs baseline: 1.0210x; 1.0210x over previous
import contextlib
import os

import numpy as np

import concourse.bass as bass
import concourse.mybir as mybir
from concourse.bass_utils import run_bass_kernel_spmd

F32 = mybir.dt.float32
BF16 = mybir.dt.bfloat16
I32 = mybir.dt.int32
AX = mybir.AxisListType.X
ALU = mybir.AluOpType
ACT = mybir.ActivationFunctionType

B = 8
DIMS = dict(S=2048, E=1024, I=2816)
TRUNC_CVT = [os.environ.get("BASS_SIM_FP64") == "1"]


def _set_dims(S, E, I):
    DIMS.update(S=S, E=E, I=I)


class _C:
    def __init__(self, h):
        self.h = h
        self.n = 0

    def inc(self, inst, k=1):
        inst.then_inc(self.h, k)
        self.n += k
        return self.n


def _chunks(total, step):
    out, off = [], 0
    while off < total:
        out.append((off, min(step, total - off)))
        off += step
    return out


def _build():
    S, E, II = DIMS["S"], DIMS["E"], DIMS["I"]
    NT, ECH, ICH = S // 128, E // 128, II // 128
    QW = min(768 if TRUNC_CVT[0] else 1024, max(E, 512))

    nc = bass.Bass("TRN2", target_bir_lowering=False, debug=False)

    x_d = nc.dram_tensor("x", [S, E], F32, kind="ExternalInput").ap()
    RRA = E // 8
    w_attn_d = {
        "q": nc.dram_tensor("wqs", [RRA, E], F32, kind="ExternalInput").ap(),
        "k": nc.dram_tensor("wks", [RRA, E], F32, kind="ExternalInput").ap(),
        "v": nc.dram_tensor("wvs", [RRA, E], F32, kind="ExternalInput").ap(),
        "p": nc.dram_tensor("wps", [RRA, E], F32, kind="ExternalInput").ap(),
    }
    WIX = {"q": 0, "k": 1, "v": 2, "p": 3}
    bp_d = nc.dram_tensor("bp", [E], F32, kind="ExternalInput").ap()
    CCG, RRD = E // 8, E // 8
    wgs_d = nc.dram_tensor("wgs", [II, CCG], F32, kind="ExternalInput").ap()
    wus_d = nc.dram_tensor("wus", [II, CCG], F32, kind="ExternalInput").ap()
    wds_d = nc.dram_tensor("wds", [RRD, II], F32, kind="ExternalInput").ap()
    out_d = nc.dram_tensor("out", [S, E], F32, kind="ExternalOutput").ap()
    scg_d = nc.dram_tensor("scg", [8, CCG, II], BF16, kind="Internal").ap()
    scu_d = nc.dram_tensor("scu", [8, CCG, II], BF16, kind="Internal").ap()
    scd_d = nc.dram_tensor("scd", [8, II, RRD], BF16, kind="Internal").ap()
    scwa_d = nc.dram_tensor("scwa", [8, 2, E, RRA], BF16, kind="Internal").ap()
    scwb_d = nc.dram_tensor("scwb", [8, 2, E, RRA], BF16, kind="Internal").ap()
    sgsl_d = nc.dram_tensor("sgsl", [CCG, II], BF16, kind="Internal").ap()
    susl_d = nc.dram_tensor("susl", [CCG, II], BF16, kind="Internal").ap()
    sdsl_d = nc.dram_tensor("sdsl", [II, RRD], BF16, kind="Internal").ap()
    swsla_d = nc.dram_tensor("swsla", [2, E, RRA], BF16, kind="Internal").ap()
    swslb_d = nc.dram_tensor("swslb", [2, E, RRA], BF16, kind="Internal").ap()
    amx_d = nc.dram_tensor("amx", [1, 4], F32, kind="Internal").ap()
    amxg_d = nc.dram_tensor("amxg", [1, 4], F32, kind="Internal").ap()
    scxq_d = nc.dram_tensor("scxq", [S, E], BF16, kind="Internal").ap()
    scbase_d = nc.dram_tensor("scbase", [S, E], F32, kind="Internal").ap()
    scpart_d = nc.dram_tensor("scpart", [S, E], F32, kind="Internal").ap()

    esL = contextlib.ExitStack()

    def sbL(es, name, shape, dt):
        return es.enter_context(nc.sbuf_tensor(name, shape, dt, side="left"))

    def sbR(es, name, shape, dt):
        return es.enter_context(nc.sbuf_tensor(name, shape, dt, side="right"))

    stat = sbL(esL, "stat", [128, 32], F32)
    bn12 = sbL(esL, "bn12", [128, 12], F32)
    ones = sbL(esL, "ones", [1, 128], F32)
    ident = sbL(esL, "ident", [128, 128], F32)
    C_MEAN, C_VAR, C_RSTD = 0, 1, 2
    C_EBIAS, C_D, C_RD = 4, 5, 6
    C_T0, C_T1, C_T2, C_T3 = 8, 9, 10, 11
    C_RS = {"q": 12, "k": 13, "v": 14, "p": 15}
    C_ALPHA, C_SVSP = 16, 17
    C_S = {"q": 20, "k": 21, "v": 22, "p": 23}
    C_GM = 24
    C_MN = 26
    C_EB = 4
    C_DD = 6
    C_RD2 = 28


    def sem(name):
        return _C(esL.enter_context(nc.semaphore(name)))

    sV, sS, sP, sG = sem("sV"), sem("sS"), sem("sP"), sem("sG")
    sIn, sWr, sT, sB = sem("sIn"), sem("sWr"), sem("sT"), sem("sB")
    sl0, sl1, sw0, sw1 = sem("sl0"), sem("sl1"), sem("sw0"), sem("sw1")
    sCC = sem("sCC")
    pt0, pt1 = sem("pt0"), sem("pt1")
    swl = [sem("swl%d" % i) for i in range(9)]
    sp0 = sem("sp0")
    swl_i = [0]
    sx0, sx1 = sem("sx0"), sem("sx1")
    sxp = [None, None]
    lsem = [sl0, sl1]
    wsem = [sw0, sw1]

    V, SC, T, SY, GP = nc.vector, nc.scalar, nc.tensor, nc.sync, nc.gpsimd

    def barrier():
        SY.wait_ge(sIn.h, sIn.n)
        SY.wait_ge(sWr.h, sWr.n)
        SY.wait_ge(sl0.h, sl0.n)
        SY.wait_ge(sl1.h, sl1.n)
        SY.wait_ge(sw0.h, sw0.n)
        SY.wait_ge(sw1.h, sw1.n)
        SC.wait_ge(sT.h, sT.n)
        SC.wait_ge(pt0.h, pt0.n)
        SC.wait_ge(pt1.h, pt1.n)
        for eng, own in ((V, sV), (SC, sS), (T, sP), (GP, sG)):
            eng.wait_ge(own.h, own.n)
            eng.sem_inc(sB.h, 1)
            sB.n += 1
        SY.sem_inc(sB.h, 1)
        sB.n += 1
        for eng in (V, SC, T, SY, GP):
            eng.wait_ge(sB.h, sB.n)

    tp = {}
    _tp_gen = [0]

    def open_temps(es, qw=None):
        qw = qw or QW
        tp.clear()
        g = _tp_gen[0] = _tp_gen[0] + 1
        names = [("z", F32, qw), ("az", F32, qw), ("c1", F32, qw),
                 ("c2", F32, qw), ("G", F32, qw), ("ti", I32, qw),
                 ("bm", F32, qw // 16), ("scl", F32, qw // 16),
                 ("rscl", F32, qw // 16)]
        if TRUNC_CVT[0]:
            names.insert(5, ("sg", F32, qw))
        for nm, dt, w in names:
            tp[nm] = sbR(es, "t_%s%d" % (nm, g), [128, w], dt)

    def fp4_qdq(src, dst, W, pre=None, P_=128):
        nb = W // 16
        b_ = lambda ap: ap.rearrange("p (nb b) -> p nb b", b=16)
        z, az, c1, c2, G = (tp[k][0:P_, :W] for k in
                            ("z", "az", "c1", "c2", "G"))
        sg = tp["sg"][0:P_, :W] if TRUNC_CVT[0] else None
        ti = tp["ti"][0:P_, :W]
        bm, scl, rscl = (tp["bm"][0:P_, :nb], tp["scl"][0:P_, :nb],
                         tp["rscl"][0:P_, :nb])
        rscl_b = rscl.unsqueeze(2).broadcast_to([P_, nb, 16])
        scl_b = scl.unsqueeze(2).broadcast_to([P_, nb, 16])
        V.wait_ge(sV.h, sV.n)
        if pre is not None:
            V.wait_ge(pre[0], pre[1])
        n = sV.inc(V.tensor_reduce(bm, b_(src), AX, ALU.max,
                                   apply_absolute_value=True))
        V.wait_ge(sV.h, n)
        n = sV.inc(V.tensor_scalar(scl, bm, 1.0 / 6.0, 1e-12, ALU.mult,
                                   ALU.max))
        V.wait_ge(sV.h, n)
        n = sV.inc(V.reciprocal(rscl, scl))
        V.wait_ge(sV.h, n)
        n = sV.inc(V.tensor_tensor(b_(z), b_(src), rscl_b, ALU.mult))
        V.wait_ge(sV.h, n)
        n = sV.inc(V.scalar_tensor_tensor(az, z, -1.0, z, ALU.mult, ALU.max))
        if TRUNC_CVT[0]:
            n = sV.inc(V.tensor_scalar(sg, z, 0.0, None, ALU.is_ge))
        V.wait_ge(sV.h, n)
        sV.inc(V.tensor_scalar(c1, az, 2.0, -1.0, ALU.is_gt, ALU.mult))
        n = sV.inc(V.tensor_scalar(c2, az, 4.0, -0.5, ALU.is_gt, ALU.mult))
        V.wait_ge(sV.h, n)
        n = sV.inc(V.scalar_tensor_tensor(G, c1, 2.0, c2, ALU.add, ALU.add))
        V.wait_ge(sV.h, n)
        if TRUNC_CVT[0]:
            n = sV.inc(V.tensor_tensor(az, z, G, ALU.mult))
            V.wait_ge(sV.h, n)
            nti = sV.inc(V.scalar_tensor_tensor(ti, sg, 0.5, az, ALU.subtract,
                                                ALU.add))
        else:
            nti = sV.inc(V.tensor_tensor(ti, z, G, ALU.mult))
        n = sV.inc(V.scalar_tensor_tensor(G, c2, 4.0, c1, ALU.mult, ALU.add))
        V.wait_ge(sV.h, n)
        n = sV.inc(V.tensor_scalar(G, G, -0.5, 0.5, ALU.mult, ALU.add))
        V.wait_ge(sV.h, n)
        n = sV.inc(V.tensor_tensor(z, ti, G, ALU.mult))
        V.wait_ge(sV.h, n)
        n = sV.inc(V.tensor_tensor(b_(dst), b_(z), scl_b, ALU.mult))
        return n

    with nc.Block() as block:

        @block.sync
        def _(_sp):
            esR = contextlib.ExitStack()
            open_temps(esR)
            mini_es = contextlib.ExitStack()
            p3b_wsa = sbR(mini_es, "p3b_wsa", [128, E], F32)

            g = sG.inc(GP.memset(ident[:], 0.0))
            GP.wait_ge(sG.h, g)
            sG.inc(GP.affine_select(out=ident[:], in_=ident[:],
                                    compare_op=ALU.not_equal, fill=1.0, base=0,
                                    pattern=[[-1, 128]], channel_multiplier=1))
            sG.inc(GP.memset(ones[:], 1.0))

            es3a = contextlib.ExitStack()
            ps3a = es3a.enter_context(nc.psum_tensor("ps3a", [128, 128], F32))
            wsa = p3b_wsa[:]
            for wi, key in enumerate(("v", "k", "q", "p")):
                if wi > 0:
                    SY.wait_ge(sV.h, sV.n)
                tgt = sIn.inc(SY.dma_start(out=wsa[0:RRA, :],
                                           in_=w_attn_d[key][0:RRA, :]), 16)
                V.wait_ge(sIn.h, tgt)
                if RRA < 128:
                    n = sV.inc(V.memset(stat[:, C_GM:C_GM + 1], 0.0))
                    V.wait_ge(sV.h, n)
                n = sV.inc(V.tensor_reduce(stat[0:RRA, C_GM:C_GM + 1],
                                           wsa[0:RRA, :], AX, ALU.max,
                                           apply_absolute_value=True))
                T.wait_ge(sV.h, n)
                T.wait_ge(sG.h, 2)
                npe = sP.inc(T.transpose(ps3a[0:1, :], stat[:, C_GM:C_GM + 1],
                                         ident[:]))
                V.wait_ge(sP.h, npe)
                n = sV.inc(V.tensor_reduce(stat[0:1, 20 + wi:21 + wi],
                                           ps3a[0:1, :], AX, ALU.max))
            SY.wait_ge(sV.h, sV.n)
            SY.wait_ge(sIn.h, sIn.n)
            namx = sWr.inc(SY.dma_start(out=amx_d, in_=stat[0:1, 20:24],
                                        allow_slow_non_contiguous=True), 16)
            GP.wait_ge(sWr.h, namx)
            sCC.inc(GP.collective_compute(
                "AllReduce", ALU.max, replica_groups=[list(range(8))],
                ins=[amx_d], outs=[amxg_d]))
            es3a.close()
            mini_es.close()

            RING = min(12, NT)
            es4L = contextlib.ExitStack()
            qT = sbL(es4L, "qT", [128, ECH, S], BF16)
            kT = sbL(es4L, "kT", [128, ECH, S], BF16)
            vt = sbL(es4L, "vt", [128, NT, E], BF16)
            es_xnT = contextlib.ExitStack()
            xnTr = sbL(es_xnT, "xnTr", [128, RING, ECH, 128], BF16)
            esM = contextlib.ExitStack()
            xnb_st = [sbR(esM, "xnb0", [128, E], BF16),
                      sbR(esM, "xnb1", [128, E], BF16)]
            xq_st = [sbR(esM, "xq0", [128, E], BF16),
                     sbR(esM, "xq1", [128, E], BF16)]
            wTcs = [sbR(esM, "wTc0", [128, ECH, 8, RRA], BF16),
                    sbR(esM, "wTc1", [128, ECH, 8, RRA], BF16)]
            xin_b = sbR(esM, "xin", [128, E], F32)
            xnf_b = sbR(esM, "xnf", [128, E], F32)
            xin = [xin_b, xin_b]
            xnf = xnf_b[:]
            esMP = contextlib.ExitStack()
            pjv = [esMP.enter_context(nc.psum_tensor("pjv%d" % k, [128, E], F32))
                   for k in range(2)]
            pkq = [esMP.enter_context(nc.psum_tensor("pkq%d" % k, [128, 512], F32))
                   for k in range(2)]
            GIX = {"v": (0, 0, 2), "k": (0, 1, 2), "q": (1, 0, 3), "p": (1, 1, 3)}

            def load_wT(buf_i, key, guard=0):
                gi, ki, cc = GIX[key]
                ws = swl[swl_i[0]]
                swl_i[0] += 1
                GP.wait_ge(sCC.h, cc)
                GP.wait_ge(sP.h, guard)
                src = (scwa_d, scwb_d)[gi][:, ki]
                n = 0
                for r in range(8):
                    if r == 4:
                        GP.wait_ge(ws.h, n)
                    n = ws.inc(GP.dma_start(
                        out=wTcs[buf_i][:, :, r, :],
                        in_=src[r].rearrange("(ec p) o -> p ec o", p=128)), 16)
                return (ws, n)

            def emit_p3b():
                es3P = contextlib.ExitStack()
                bc3 = es3P.enter_context(nc.psum_tensor("bc3", [128, 4], F32))
                SY.wait_ge(sCC.h, 1)
                namg = sIn.inc(SY.dma_start(out=stat[0:1, 24:28], in_=amxg_d,
                                                    allow_slow_non_contiguous=True), 16)
                T.wait_ge(sIn.h, namg)
                T.wait_ge(sG.h, 3)
                npe = sP.inc(T.matmul(bc3[:, 0:4], ones[:], stat[0:1, 24:28],
                                          start=True, stop=True))
                V.wait_ge(sP.h, npe)
                n = sV.inc(V.reciprocal(stat[:, 12:16], bc3[:, 0:4]))
                V.wait_ge(sV.h, n)
                n = sV.inc(V.tensor_scalar(stat[:, 12:16], stat[:, 12:16],
                                                   127.0, None, ALU.mult))
                n = sV.inc(V.tensor_scalar(stat[:, 20:24], bc3[:, 0:4],
                                                   1.0 / 127.0, None, ALU.mult))
                C_RS = {"v": 12, "k": 13, "q": 14, "p": 15}
                C_S = {"v": 20, "k": 21, "q": 22, "p": 23}
                V.wait_ge(sV.h, n)
                n = sV.inc(V.tensor_tensor(stat[:, C_ALPHA:C_ALPHA + 1],
                                                   stat[:, C_S["q"]:C_S["q"] + 1],
                                                   stat[:, C_S["k"]:C_S["k"] + 1], ALU.mult))
                V.wait_ge(sV.h, n)
                sV.inc(V.tensor_scalar(stat[:, C_ALPHA:C_ALPHA + 1],
                                           stat[:, C_ALPHA:C_ALPHA + 1], 0.125, None,
                                           ALU.mult))
                sV.inc(V.tensor_tensor(stat[:, C_SVSP:C_SVSP + 1],
                                           stat[:, C_S["v"]:C_S["v"] + 1],
                                           stat[:, C_S["p"]:C_S["p"] + 1], ALU.mult))
                wsa3 = xnf[:]
                wqst = tp["c1"][:].bitcast(BF16)[:, 0:E]
                wslt = tp["c2"][:].bitcast(BF16)[:, 0:ECH * RRA]
                last_wsl_store = [None]
                for gi, group in enumerate((("v", "k"), ("q", "p"))):
                    gsem = (pt0, pt1)[gi]
                    for ki, key in enumerate(group):
                        SY.wait_ge(sV.h, sV.n)
                        SY.wait_ge(sT.h, sT.n)
                        tgt = sIn.inc(SY.dma_start(out=wsa3[0:RRA, :],
                                                           in_=w_attn_d[key][0:RRA, :]), 16)
                        V.wait_ge(sIn.h, tgt)
                        V.wait_ge(gsem.h, gsem.n)
                        rs_ap = stat[0:RRA, C_RS[key]:C_RS[key] + 1]
                        for qoff, qw in _chunks(E, QW):
                            tz = tp["z"][0:RRA, :qw]
                            tti = tp["ti"][0:RRA, :qw]
                            src = wsa3[0:RRA, qoff:qoff + qw]
                            V.wait_ge(sV.h, sV.n)
                            if TRUNC_CVT[0]:
                                tsg = tp["sg"][0:RRA, :qw]
                                sV.inc(V.tensor_scalar(tsg, src, 0.0, None,
                                                       ALU.is_ge))
                                n2 = sV.inc(V.tensor_scalar(tz, src, rs_ap,
                                                            None, ALU.mult))
                                V.wait_ge(sV.h, n2)
                                n = sV.inc(V.scalar_tensor_tensor(
                                    tti, tsg, 0.5, tz, ALU.subtract, ALU.add))
                            else:
                                n = sV.inc(V.tensor_scalar(tti, src, rs_ap,
                                                           None, ALU.mult))
                            V.wait_ge(sV.h, n)
                            n = sV.inc(V.tensor_copy(
                                wqst[0:RRA, qoff:qoff + qw], tti))
                        SC.wait_ge(sV.h, n)
                        if last_wsl_store[0] is not None:
                                SC.wait_ge(last_wsl_store[0].h, last_wsl_store[1])
                        nt_ = sT.inc(SC.dma_start_transpose(
                                wslt[:].rearrange("p (ec o) -> p ec o", o=RRA),
                                wqst[0:RRA, :].rearrange("p (ec e) -> p ec e", e=128)), 16)
                        SY.wait_ge(sT.h, nt_)
                        gsem.inc(SY.dma_start(
                                out=(swsla_d, swslb_d)[gi][ki].rearrange(
                                    "(ec p) o -> p ec o", p=128),
                                in_=wslt[:].rearrange("p (ec o) -> p ec o", o=RRA)), 16)
                        last_wsl_store[0], last_wsl_store[1:] = gsem, [gsem.n]
                    GP.wait_ge(gsem.h, gsem.n)
                    sCC.inc(GP.collective_compute(
                        "AllGather", ALU.bypass, replica_groups=[list(range(8))],
                        ins=[(swsla_d, swslb_d)[gi]], outs=[(scwa_d, scwb_d)[gi]]))
                SY.wait_ge(sIn.h, sIn.n)
                V.wait_ge(sT.h, sT.n)
                V.wait_ge(pt0.h, pt0.n)
                V.wait_ge(pt1.h, pt1.n)
                es3P.close()


            wv_ready = [0]
            kq_done = [0]
            ln_done = [0]
            copy_last = [0]
            vevac_done = [0, 0]
            ring_free = {}
            MSL_E = _chunks(E, 512)
            kq_evac = [0, 0]

            def kq_prod(proj, g, buf_i, wld):
                dstT = kT if proj == "k" else qT
                T.wait_ge(wld[0].h, wld[1])
                step = 0
                for oc in range(ECH):
                    for (s0, sw) in _chunks(S // 2, 512):
                        pk = pkq[step % 2]
                        T.wait_ge(sS.h, kq_evac[step % 2])
                        a0 = g * (S // 2) + s0
                        rs = (a0 // 128) % RING
                        last = None
                        for ec in range(ECH):
                            st = (wTcs[buf_i][:, ec, oc, :] if RRA == 128 else
                                  wTcs[buf_i][:, ec, oc * (128 // RRA):(oc + 1) * (128 // RRA), :])
                            last = T.matmul(
                                pk[:, 0:sw], st,
                                xnTr[:, rs:rs + sw // 128, ec, :],
                                start=(ec == 0), stop=(ec == ECH - 1))
                        npe = sP.inc(last)
                        SC.wait_ge(sP.h, npe)
                        kq_evac[step % 2] = sS.inc(
                            SC.activation(dstT[:, oc, a0:a0 + sw],
                                          pk[:, 0:sw], ACT.Copy))
                        step += 1
                kq_done[0] = sP.n

            emit_p3b()
            wv_ready[0] = load_wT(0, "v")
            if NT > 9:
                sched = {7: [("k", 0)], 9: [("q", 0)]}
            else:
                sched = {NT - 1: [("k", 0), ("q", 0)]}
            xload_done = [0] * NT

            def xload(t):
                if t >= 1:
                    SY.wait_ge(sV.h, ln_done[0])
                xload_done[t] = lsem[t % 2].inc(SY.dma_start(
                    out=xin_b[:], in_=x_d[t * 128:(t + 1) * 128, :]), 16)

            xload(0)

            def ln_stats(t):
                p = t % 2
                mv0 = 2 * p
                V.wait_ge(lsem[p].h, xload_done[t])
                sV.inc(V.bn_stats(bn12[:, 0:6], xin_b[:, 0:E // 2]))
                n = sV.inc(V.bn_stats(bn12[:, 6:12], xin_b[:, E // 2:E]))
                V.wait_ge(sV.h, n)
                n = sV.inc(V.bn_aggr(stat[:, mv0:mv0 + 2], bn12[:]))
                V.wait_ge(sV.h, n)
                n = sV.inc(V.tensor_scalar(stat[:, 8 + p:9 + p],
                                           stat[:, mv0 + 1:mv0 + 2], 1e-5,
                                           None, ALU.add))
                SC.wait_ge(sV.h, n)
                return sS.inc(SC.activation(stat[:, 10 + p:11 + p],
                                            stat[:, 8 + p:9 + p], ACT.Sqrt))

            def ln_newton(t, m):
                p = t % 2
                T0, T1, T2, T3 = (stat[:, c + p:c + p + 1]
                                  for c in (6, 8, 10, 12))
                V.wait_ge(sS.h, m)
                n = sV.inc(V.reciprocal(T0, T2))
                V.wait_ge(sV.h, n)
                n = sV.inc(V.tensor_tensor(T3, T0, T0, ALU.mult))
                V.wait_ge(sV.h, n)
                n = sV.inc(V.tensor_tensor(T3, T3, T1, ALU.mult))
                V.wait_ge(sV.h, n)
                n = sV.inc(V.tensor_scalar(T3, T3, -0.5, 1.5, ALU.mult,
                                           ALU.add))
                V.wait_ge(sV.h, n)
                n = sV.inc(V.tensor_tensor(stat[:, 4 + p:5 + p], T0, T3,
                                           ALU.mult))
                V.wait_ge(sV.h, n)

            ln_newton(0, ln_stats(0))
            for t in range(NT):
                sl = t % 2
                sxs = (sx0, sx1)[sl]
                if t >= 1:
                    V.wait_ge(sS.h, copy_last[0])
                n = sV.inc(V.tensor_scalar(xnf[:], xin_b[:],
                                           stat[:, 2 * sl:2 * sl + 1],
                                           stat[:, 4 + sl:5 + sl],
                                           ALU.subtract, ALU.mult))
                ln_done[0] = n
                if t + 1 < NT:
                    xload(t + 1)
                SC.wait_ge(sV.h, n)
                SC.wait_ge(sxs.h, sxs.n)
                m = sS.inc(SC.activation(xnb_st[sl][:], xnf[:], ACT.Copy))
                copy_last[0] = m
                SC.wait_ge(sS.h, m)
                slot = t % RING
                if slot in ring_free:
                    SC.wait_ge(sP.h, ring_free[slot])
                nxt = sxs.inc(SC.dma_start_transpose(
                    xnTr[:, slot],
                    xnb_st[sl][:].rearrange("p (ec e) -> p ec e", e=128)), 16)
                V.wait_ge(wsem[sl].h, wsem[sl].n)
                fin = 0
                for off, W in _chunks(E, QW):
                    fin = fp4_qdq(xnf[:, off:off + W], xq_st[sl][:, off:off + W], W)
                SY.wait_ge(sV.h, fin)
                wsem[sl].inc(SY.dma_start(
                    out=scxq_d[t * 128:(t + 1) * 128, :], in_=xq_st[sl][:]), 16)
                if t + 1 < NT:
                    sq_next = ln_stats(t + 1)
                pv = pjv[sl]
                T.wait_ge(wv_ready[0][0].h, wv_ready[0][1])
                T.wait_ge(sxs.h, nxt)
                T.wait_ge(sS.h, vevac_done[sl])
                last = None
                for ec in range(ECH):
                    st = xnTr[:, slot, ec, :]
                    for (o0, ow) in MSL_E:
                        last = T.matmul(pv[:, o0:o0 + ow], st,
                                        (wTcs[0][:, ec, o0 // RRA:(o0 + ow) // RRA, :]),
                                        start=(ec == 0), stop=(ec == ECH - 1))
                npe = sP.inc(last)
                SC.wait_ge(sP.h, npe)
                vevac_done[sl] = sS.inc(
                    SC.activation(vt[:, t, :], pv[:, 0:E], ACT.Copy))
                if t + 1 < NT:
                    ln_newton(t + 1, sq_next)
                for (proj, g_) in sched.get(t, []):
                    wld = load_wT(1, proj, guard=kq_done[0])
                    kq_prod(proj, g_, 1, wld)
                    if NT > RING:
                        ring_free.update({s_: kq_done[0] for s_ in range(8)})
            for (proj, g_) in (("k", 1), ("q", 1)):
                wld = load_wT(1, proj, guard=kq_done[0])
                kq_prod(proj, g_, 1, wld)
            barrier()
            esMP.close()
            esM.close()
            es_xnT.close()
            es7a = contextlib.ExitStack()
            wpTT = sbL(es7a, "wpTT", [128, ECH, 8, RRA], BF16)
            bb = sbL(es7a, "bb", [128, E], F32)
            es5P = contextlib.ExitStack()
            sco = [es5P.enter_context(nc.psum_tensor("scoA", [128, max(S, 2 * E)], F32)),
                   es5P.enter_context(nc.psum_tensor("scoB", [128, max(S, 2 * E)], F32))]
            es5R = contextlib.ExitStack()
            p_bf = [sbR(es5R, "p_bf0", [128, S], BF16),
                    sbR(es5R, "p_bf1", [128, S], BF16)]
            pTb = [sbR(es5R, "pT0", [128, NT, 128], BF16),
                   sbR(es5R, "pT1", [128, NT, 128], BF16)]
            avb = [sbR(es5R, "av0", [128, E], BF16),
                   sbR(es5R, "av1", [128, E], BF16)]
            avT_t = [sbR(es5R, "avT0", [128, ECH, 128], BF16),
                     sbR(es5R, "avT1", [128, ECH, 128], BF16)]
            xrl = [sbR(es5R, "xr0", [128, E], F32), sbR(es5R, "xr1", [128, E], F32)]
            baseb = [sbR(es5R, "bs0", [128, E], F32), sbR(es5R, "bs1", [128, E], F32)]
            wsl_in = sbR(es5R, "wsl_in", [128, QW], F32)
            wsl_out = sbR(es5R, "wsl_out", [128, QW], BF16)
            wsl_t = sbR(es5R, "wsl_t", [128, QW], BF16)
            SY.wait_ge(sCC.h, 3)
            SY.wait_ge(sIn.h, sIn.n)
            nld7 = 0
            for r in range(8):
                if r % 3 == 0 and r:
                    SY.wait_ge(sIn.h, sIn.n)
                nld7 = sIn.inc(SY.dma_start(
                    out=wpTT[:, :, r, :],
                    in_=scwb_d[r, 1].rearrange("(ec p) o -> p ec o", p=128)), 16)
            nbb = sIn.inc(SY.dma_start(out=bb[:],
                                       in_=bp_d[None, :].broadcast_to([128, E])), 16)
            nld7 = nbb
            SY.wait_ge(sIn.h, sIn.n)
            if CCG < 128:
                p0_memset = sV.inc(V.memset(wsl_in[:], 0.0))
            else:
                p0_memset = 0
            p0_units = []
            for (src_d, dst_d, kind) in ((wgs_d, sgsl_d, "col"),
                                         (wus_d, susl_d, "col"),
                                         (wds_d, sdsl_d, "row")):
                for off, W in _chunks(II, QW):
                    p0_units.append((src_d, dst_d, kind, off, W))
            p0_state = dict(ld=0, st=0, rd=p0_memset, tp=0, cur=None,
                            pend=None)

            def p0_flush():
                P_, WW, cw, ntt, dst_d, off, W, kind, fin = p0_state["pend"]
                p0_state["pend"] = None
                SC.wait_ge(sV.h, fin)
                SC.wait_ge(sWr.h, p0_state["st"])
                SC.wait_ge(sT.h, p0_state["tp"])
                p0_state["tp"] = sT.inc(SC.dma_start_transpose(
                    wsl_t[:, 0:ntt * P_].rearrange("p (nt o) -> p nt o",
                                                   nt=ntt),
                    wsl_out[0:P_, 0:WW].rearrange("p (nt c) -> p nt c",
                                                  nt=ntt)), 16)
                SY.wait_ge(sT.h, p0_state["tp"])
                if kind == "col":
                    outv = dst_d[:, off:off + W].rearrange(
                        "c (nt n) -> c nt n", n=128)
                else:
                    outv = dst_d[off:off + W, :].rearrange(
                        "(nt n) o -> n nt o", n=128)
                p0_state["st"] = sWr.inc(SY.dma_start(
                    out=outv,
                    in_=wsl_t[0:cw, 0:ntt * P_].rearrange(
                        "p (nt o) -> p nt o", nt=ntt)), 16)

            def emit_p0_unit(u):
                if p0_state["pend"]:
                    p0_flush()
                src_d, dst_d, kind, off, W = u
                ntt = W // 128
                SY.wait_ge(sV.h, p0_state["rd"])
                SY.wait_ge(sp0.h, p0_state["ld"])
                if kind == "col":
                    P_, WW, cw = 128, ntt * 128, CCG
                    inv = src_d[off:off + W, :].rearrange(
                        "(nt p) c -> p nt c", p=128)
                    ldv = wsl_in[0:128, 0:WW].rearrange(
                        "p (nt c) -> p nt c", c=128)[:, :, 0:CCG]
                else:
                    P_, WW, cw = RRD, W, 128
                    inv = src_d[0:RRD, off:off + W]
                    ldv = wsl_in[0:RRD, 0:W]
                p0_state["ld"] = sp0.inc(SY.dma_start(out=ldv, in_=inv), 16)
                V.wait_ge(sWr.h, p0_state["st"])
                V.wait_ge(sT.h, p0_state["tp"])
                fin = fp4_qdq(wsl_in[0:P_, 0:WW], wsl_out[0:P_, 0:WW], WW,
                              pre=(sp0.h, p0_state["ld"]), P_=P_)
                p0_state["rd"] = fin
                p0_state["pend"] = (P_, WW, cw, ntt, dst_d, off, W, kind, fin)

            exp_done = [0, 0]
            avevac_done = [0, 0]
            avmm_done = [0, 0]
            pT_done = [0, 0]
            rd_done = [0, 0]
            avT_done = [0, 0]
            proj_done = [0, 0]
            base_done = [0, 0]
            basewr_done = [0, 0]
            def emit_sc_part(i):
                sl = i % 2
                p = sco[sl]
                T.wait_ge(sS.h, exp_done[sl])
                T.wait_ge(sS.h, avevac_done[sl])
                T.wait_ge(sV.h, base_done[sl])
                last = None
                for oc in range(ECH):
                    st = qT[:, oc, i * 128:(i + 1) * 128]
                    for (s0, sw) in _chunks(S, 512):
                        last = T.matmul(p[:, s0:s0 + sw], st,
                                        kT[:, oc, s0:s0 + sw],
                                        start=(oc == 0), stop=(oc == ECH - 1))
                nsc = sP.inc(last)
                cMN, cEB, cDD, cRD = C_MN + sl, C_EB + sl, C_DD + sl, C_RD2 + sl
                V.wait_ge(sP.h, nsc)
                V.wait_ge(sS.h, exp_done[sl])
                n = sV.inc(V.reduce_max(stat[:, cMN:cMN + 1], p[:, 0:S], AX,
                                        negate=True))
                V.wait_ge(sV.h, n)
                n = sV.inc(V.tensor_tensor(stat[:, cEB:cEB + 1],
                                           stat[:, cMN:cMN + 1],
                                           stat[:, C_ALPHA:C_ALPHA + 1], ALU.mult))
                SC.wait_ge(sV.h, n)
                SC.wait_ge((pt0, pt1)[sl].h, pT_done[sl])
                m = sS.inc(SC.activation(p_bf[sl][:], p[:, 0:S], ACT.Exp,
                                         bias=stat[:, cEB:cEB + 1],
                                         scale=stat[:, C_ALPHA:C_ALPHA + 1],
                                         accum_out=stat[:, cDD:cDD + 1]))
                exp_done[sl] = m
                V.wait_ge(sS.h, m)
                V.wait_ge(sS.h, avevac_done[sl])
                n = sV.inc(V.reciprocal(stat[:, cRD:cRD + 1],
                                        stat[:, cDD:cDD + 1]))
                rd_done[sl] = n
                SC.wait_ge(sS.h, exp_done[sl])
                SC.wait_ge(sP.h, avmm_done[sl])
                ptsem = (pt0, pt1)[sl]
                pT_done[sl] = ptsem.inc(SC.dma_start_transpose(
                    pTb[sl][:],
                    p_bf[sl][:].rearrange("p (t s) -> p t s", s=128)), 16)
            def emit_av_part(i):
                sl = i % 2
                p = sco[sl]
                T.wait_ge(sS.h, exp_done[sl])
                T.wait_ge((pt0, pt1)[sl].h, pT_done[sl])
                T.wait_ge(sS.h, avevac_done[sl])
                last = None
                for kc in range(NT):
                    st = pTb[sl][:, kc, :]
                    for (o0, ow) in _chunks(E, 512):
                        last = T.matmul(p[:, o0:o0 + ow], st,
                                        vt[:, kc, o0:o0 + ow],
                                        start=(kc == 0), stop=(kc == NT - 1))
                nav = sP.inc(last)
                avmm_done[sl] = nav
                SC.wait_ge(sP.h, nav)
                SC.wait_ge(sV.h, rd_done[sl])
                sxs5 = (sx0, sx1)[sl]
                SC.wait_ge(sxs5.h, sxs5.n)
                avevac_done[sl] = sS.inc(SC.activation(
                    avb[sl][:], p[:, 0:E], ACT.Copy,
                    scale=stat[:, C_RD2 + sl:C_RD2 + sl + 1]))
                SC.wait_ge(sS.h, avevac_done[sl])
                SC.wait_ge(sP.h, proj_done[sl])
                avT_done[sl] = sxs5.inc(SC.dma_start_transpose(
                    avT_t[sl][:],
                    avb[sl][:].rearrange("p (t n) -> p t n", n=128)), 16)
                SY.wait_ge(sV.h, base_done[sl])
                nxr = lsem[sl].inc(SY.dma_start(
                    out=xrl[sl][:], in_=x_d[i * 128:(i + 1) * 128, :]), 16)
                T.wait_ge(sxs5.h, avT_done[sl])
                T.wait_ge(sIn.h, nld7)
                last = None
                for ec in range(ECH):
                    st = avT_t[sl][:, ec, :]
                    for (o0, ow) in _chunks(E, 512):
                        last = T.matmul(p[:, E + o0:E + o0 + ow], st,
                                        wpTT[:, ec, o0 // RRA:(o0 + ow) // RRA, :],
                                        start=(ec == 0), stop=(ec == ECH - 1))
                npr = sP.inc(last)
                proj_done[sl] = npr
                V.wait_ge(sP.h, npr)
                V.wait_ge(lsem[sl].h, nxr)
                V.wait_ge(sV.h, sV.n)
                n = sV.inc(V.scalar_tensor_tensor(
                    xrl[sl][:], p[:, E:2 * E], stat[:, C_SVSP:C_SVSP + 1],
                    xrl[sl][:], ALU.mult, ALU.add))
                V.wait_ge(sV.h, n)
                V.wait_ge(sIn.h, nbb)
                V.wait_ge(wsem[sl].h, basewr_done[sl])
                n = sV.inc(V.tensor_tensor(baseb[sl][:], xrl[sl][:], bb[:],
                                           ALU.add))
                base_done[sl] = n
                SY.wait_ge(sV.h, n)
                basewr_done[sl] = wsem[sl].inc(SY.dma_start(
                    out=scbase_d[i * 128:(i + 1) * 128, :], in_=baseb[sl][:]), 16)
            def emit_p0_batch(i):
                    rem = max(1, (NT * 3 // 4) - i)
                    nu = (len(p0_units) + rem - 1) // rem
                    for _k in range(nu):
                        if p0_units:
                            emit_p0_unit(p0_units.pop(0))
                    if not p0_units and p0_state["cur"] != "done":
                        p0_state["cur"] = "done"
                        if p0_state["pend"]:
                            p0_flush()
                        GP.wait_ge(sWr.h, sWr.n)
                        sCC.inc(GP.collective_compute(
                            "AllGather", ALU.bypass,
                            replica_groups=[list(range(8))],
                            ins=[sgsl_d], outs=[scg_d]))
                        sCC.inc(GP.collective_compute(
                            "AllGather", ALU.bypass,
                            replica_groups=[list(range(8))],
                            ins=[susl_d], outs=[scu_d]))
                        sCC.inc(GP.collective_compute(
                            "AllGather", ALU.bypass,
                            replica_groups=[list(range(8))],
                            ins=[sdsl_d], outs=[scd_d]))

            for i in range(NT + 1):
                if i < NT:
                    emit_sc_part(i)
                if i >= 1:
                    emit_av_part(i - 1)
                if i < NT:
                    emit_p0_batch(i)
            es5R.close()
            barrier()
            es5P.close()
            es7a.close()
            es4L.close()

            if (II // 2) % 128 == 0 and II > 1024:
                HALVES = [(0, II // 2), (II // 2, II // 2)]
            else:
                HALVES = [(0, II)]
            es8L = contextlib.ExitStack()
            IHX = HALVES[0][1]
            ICH2 = IHX // 128
            CH8 = [(0, IHX)]
            CW8 = IHX
            wgH = sbL(es8L, "wgH", [128, ECH, IHX], BF16)
            wuH = sbL(es8L, "wuH", [128, ECH, IHX], BF16)
            wdH = sbL(es8L, "wdH", [128, ICH2, 8, RRD], BF16)
            es8R = contextlib.ExitStack()
            xqT_t = [sbR(es8R, "xqt0", [128, ECH, 128], BF16),
                     sbR(es8R, "xqt1", [128, ECH, 128], BF16)]
            hq = [sbR(es8R, "hq0", [128, IHX], BF16),
                  sbR(es8R, "hq1", [128, IHX], BF16)]
            hqT = [sbR(es8R, "hqT0", [128, ICH2, 128], BF16),
                   sbR(es8R, "hqT1", [128, ICH2, 128], BF16)]
            sgl = [sbR(es8R, "sgl0", [128, CW8], F32),
                   sbR(es8R, "sgl1", [128, CW8], F32)]
            hbuf = sbR(es8R, "hbuf", [128, CW8], F32)
            rlb = [sbR(es8R, "rl0", [128, E], F32), sbR(es8R, "rl1", [128, E], F32)]
            stb = [sbR(es8R, "st0", [128, E], F32), sbR(es8R, "st1", [128, E], F32)]
            open_temps(es8R, qw=CW8)
            es8P = contextlib.ExitStack()
            pg = es8P.enter_context(nc.psum_tensor("pg", [128, CW8], F32))
            py = es8P.enter_context(nc.psum_tensor("py", [128, CW8], F32))
            pdn0 = es8P.enter_context(nc.psum_tensor("pdn0", [128, E], F32))
            scg_re = scg_d.rearrange("(ec rr) q n -> (rr q) ec n", ec=ECH)
            scu_re = scu_d.rearrange("(ec rr) q n -> (rr q) ec n", ec=ECH)
            xq_rd = [0, 0]
            nxq_p = [0, 0]
            wg_last = [0]
            wu_last = [0]
            wd_last = [0]
            silu_prev = [0]
            h_prev = [0]
            h_prev2 = [0, 0]
            hqtp_done = [0, 0]
            hqtp_done2 = [0, 0]
            rl_n = [0, 0]
            rl_n2 = [0, 0]
            down_done = [0, 0]
            f_done = [0, 0]
            f_last = [0]
            half_wu = [None]
            half_wd = [None]
            stwr_done = [0, 0]
            cc8 = [0]
            for hi, (IH0, IHW) in enumerate(HALVES):
                lasth = hi == len(HALVES) - 1
                rl_src = scbase_d if hi == 0 else scpart_d
                st_dst = out_d if lasth else scpart_d
                SY.wait_ge(sIn.h, sIn.n)
                SY.wait_ge(sCC.h, 6)
                SY.wait_ge(sP.h, wg_last[0])
                sIn.inc(SY.dma_start(out=wgH[:],
                                     in_=scg_re[:, :, IH0:IH0 + IHW]), 16)
                SY.wait_ge(sP.h, wu_last[0])
                swu = swl[swl_i[0]]
                swl_i[0] += 1
                nwu = swu.inc(SY.dma_start(out=wuH[:],
                                           in_=scu_re[:, :, IH0:IH0 + IHW]), 16)
                half_wu[0] = (swu, nwu)
                wdsem = swl[swl_i[0]]
                swl_i[0] += 1
                GP.wait_ge(sCC.h, 6)
                GP.wait_ge(sP.h, wd_last[0])
                nwl = 0
                for r in range(8):
                    if r == 4:
                        GP.wait_ge(wdsem.h, nwl)
                    nwl = swl_i and wdsem.inc(GP.dma_start(
                        out=wdH[:, :, r, :],
                        in_=scd_d[r, IH0:IH0 + IHW, :].rearrange(
                            "(nc p) o -> p nc o", p=128)), 16)
                half_wd[0] = (wdsem, nwl)
                SY.wait_ge(sIn.h, sIn.n)
                SY.wait_ge(sw0.h, sw0.n)
                SY.wait_ge(sw1.h, sw1.n)
                SY.wait_ge(swu.h, nwu)
                T.wait_ge(sIn.h, sIn.n)
                def emit_xqt(sc):
                    sl = sc % 2
                    sxs8 = (sx0, sx1)[sl]
                    SC.wait_ge(sxs8.h, sxs8.n)
                    SC.wait_ge(sP.h, xq_rd[sl])
                    nxq_p[sl] = sxs8.inc(SC.dma_start_transpose(
                        xqT_t[sl][:],
                        scxq_d[sc * 128:(sc + 1) * 128, :]
                        .rearrange("p (ec e) -> p ec e", e=128)), 16)

                def emit_front(sc):
                    sl = sc % 2
                    sxs8 = (sx0, sx1)[sl]
                    if sc == 0:
                        T.wait_ge(half_wu[0][0].h, half_wu[0][1])
                    T.wait_ge(sxs8.h, nxq_p[sl])
                    hq_fin = 0
                    for (off, W) in CH8:
                        nw = [(off + o, min(512, W - o))
                              for o in range(0, W, 512)]
                        T.wait_ge(sS.h, silu_prev[0])
                        last = None
                        for ec in range(ECH):
                            st = xqT_t[sl][:, ec, :]
                            for (w0, ww) in nw:
                                last = T.matmul(
                                    pg[:, w0 - off:w0 - off + ww], st,
                                    wgH[:, ec, w0:w0 + ww],
                                    start=(ec == 0), stop=(ec == ECH - 1))
                        ng = sP.inc(last)
                        wg_last[0] = ng
                        T.wait_ge(sV.h, h_prev[0])
                        last = None
                        for ec in range(ECH):
                            st = xqT_t[sl][:, ec, :]
                            for (w0, ww) in nw:
                                last = T.matmul(
                                    py[:, w0 - off:w0 - off + ww], st,
                                    wuH[:, ec, w0:w0 + ww],
                                    start=(ec == 0), stop=(ec == ECH - 1))
                        ny = sP.inc(last)
                        wu_last[0] = ny
                        xq_rd[sl] = ny
                        pp = cc8[0] % 2
                        SC.wait_ge(sP.h, ng)
                        SC.wait_ge(sV.h, h_prev2[pp])
                        ns = sS.inc(SC.activation(sgl[pp][:, 0:W], pg[:, 0:W],
                                                  ACT.Silu))
                        silu_prev[0] = ns
                        V.wait_ge(sP.h, ny)
                        V.wait_ge(sS.h, ns)
                        V.wait_ge(sV.h, sV.n)
                        nh = sV.inc(V.scalar_tensor_tensor(
                            hbuf[:, 0:W], py[:, 0:W], 1.0, sgl[pp][:, 0:W],
                            ALU.mult, ALU.mult))
                        h_prev[0] = nh
                        h_prev2[pp] = nh
                        V.wait_ge(sT.h, hqtp_done[sl])
                        hq_fin = fp4_qdq(hbuf[:, 0:W], hq[sl][:, off:off + W],
                                         W)
                        cc8[0] += 1
                    if sc + 1 < NT:
                        emit_xqt(sc + 1)
                    SC.wait_ge(sV.h, hq_fin)
                    SC.wait_ge(sT.h, sT.n)
                    SC.wait_ge(sP.h, down_done[sl])
                    hqtp_done[sl] = sT.inc(SC.dma_start_transpose(
                        hqT[sl][:],
                        hq[sl][:].rearrange("p (t n) -> p t n", n=128)), 16)
                    SY.wait_ge(sV.h, f_done[sl])
                    rl_n[sl] = lsem[sl].inc(SY.dma_start(
                        out=rlb[sl][:],
                        in_=rl_src[sc * 128:(sc + 1) * 128, :]), 16)

                def emit_back(sc):
                    sl = sc % 2
                    if sc == 0:
                        T.wait_ge(half_wd[0][0].h, half_wd[0][1])
                    T.wait_ge(sT.h, hqtp_done2[sl])
                    T.wait_ge(sV.h, f_last[0])
                    last = None
                    for ic in range(ICH2):
                        st = hqT[sl][:, ic, :]
                        for (o0, ow) in _chunks(E, 512):
                            last = T.matmul(
                                pdn0[:, o0:o0 + ow], st,
                                wdH[:, ic, o0 // RRD:(o0 + ow) // RRD, :],
                                start=(ic == 0), stop=(ic == ICH2 - 1))
                    ndn = sP.inc(last)
                    down_done[sl] = ndn
                    wd_last[0] = ndn
                    V.wait_ge(sP.h, ndn)
                    V.wait_ge(lsem[sl].h, rl_n2[sl])
                    V.wait_ge(wsem[sl].h, stwr_done[sl])
                    V.wait_ge(sV.h, sV.n)
                    nf = sV.inc(V.scalar_tensor_tensor(
                        stb[sl][:], pdn0[:, 0:E], 1.0, rlb[sl][:],
                        ALU.mult, ALU.add))
                    f_done[sl] = nf
                    f_last[0] = nf
                    SY.wait_ge(sV.h, nf)
                    stwr_done[sl] = wsem[sl].inc(SY.dma_start(
                        out=st_dst[sc * 128:(sc + 1) * 128, :],
                        in_=stb[sl][:]), 16)

                emit_xqt(0)
                for sc in range(NT + 1):
                    if sc < NT:
                        emit_front(sc)
                    if sc >= 1:
                        hqtp_done2[(sc - 1) % 2] = hqtp_done[(sc - 1) % 2]
                        rl_n2[(sc - 1) % 2] = rl_n[(sc - 1) % 2]
                        emit_back(sc - 1)
            SY.wait_ge(sw0.h, sw0.n)
            SY.wait_ge(sw1.h, sw1.n)
            es8P.close()
            es8R.close()
            es8L.close()

    esL.close()
    return nc


_NC_CACHE = [None]


def kernel(**inputs):
    x = np.ascontiguousarray(np.asarray(inputs["L0_attn_x"], dtype=np.float32))
    wq = np.asarray(inputs["L0_attn_w_query"], np.float32)
    wk = np.asarray(inputs["L0_attn_w_key"], np.float32)
    wv = np.asarray(inputs["L0_attn_w_value"], np.float32)
    wp = np.asarray(inputs["L0_attn_w_proj"], np.float32)
    bp = np.ascontiguousarray(np.asarray(inputs["L0_attn_b_proj"], np.float32))
    wg = np.asarray(inputs["L0_ffn_w_gate"], np.float32)
    wu = np.asarray(inputs["L0_ffn_w_up"], np.float32)
    wd = np.asarray(inputs["L0_ffn_w_down"], np.float32)
    rra = wq.shape[0] // B
    ccg, rrd = wg.shape[1] // B, wd.shape[0] // B
    if _NC_CACHE[0] is None:
        _NC_CACHE[0] = _build()
    nc = _NC_CACHE[0]
    A = np.ascontiguousarray
    in_maps = [dict(x=A(x[c]), bp=bp,
                    wqs=A(wq[c * rra:(c + 1) * rra]),
                    wks=A(wk[c * rra:(c + 1) * rra]),
                    wvs=A(wv[c * rra:(c + 1) * rra]),
                    wps=A(wp[c * rra:(c + 1) * rra]),
                    wgs=A(wg[:, c * ccg:(c + 1) * ccg]),
                    wus=A(wu[:, c * ccg:(c + 1) * ccg]),
                    wds=A(wd[c * rrd:(c + 1) * rrd]))
               for c in range(B)]
    res = run_bass_kernel_spmd(nc, in_maps, core_ids=list(range(B)))
    out = np.stack([np.asarray(res.results[c]["out"]) for c in range(B)], axis=0)
    return out.astype(np.float32)
